# version 10
# baseline (speedup 1.0000x reference)
import numpy as np
import concourse.bass as bass
import concourse.mybir as mybir
from concourse.bass_utils import run_bass_kernel_spmd
from contextlib import ExitStack

F32 = mybir.dt.float32
BF16 = mybir.dt.bfloat16
AF = mybir.ActivationFunctionType
ALU = mybir.AluOpType

D = 1024
DFF = 4096
NH, HD = 16, 64
EPS = 1e-5

CFG_FULL = dict(T=512, n_strips=32, own_strips=8)

ENGS = ("sync", "gp", "pe", "dv", "ac")


class Rec:
    def __init__(self):
        self.items = {k: [] for k in ENGS}
        self.val = 0
        self.ev = 0
        self.pending = []
        self.defer = False

    def emit(self, eng, fn, inc, pre=(), post=(), evflag=False, tkwait=True):
        if self.defer:
            self.pending.append((eng, fn, inc, tuple(pre), tuple(post), evflag, tkwait))
            return
        self._commit(eng, fn, inc, pre, post, evflag, tkwait)

    def _commit(self, eng, fn, inc, pre, post, evflag, tkwait):
        if evflag:
            self.ev += 1
        self.items[eng].append((self.val if tkwait else None, fn, inc,
                                tuple(pre), tuple(post)))
        self.val += inc

    def flush(self, k):
        n = 0
        while self.pending and n < k:
            (eng, fn, inc, pre, post, evflag, tkwait) = self.pending.pop(0)
            self._commit(eng, fn, inc, pre, post, evflag, tkwait)
            n += 1

    def flush_all(self):
        self.flush(len(self.pending))


class EngRec:
    def __init__(self, rec, name, inc):
        object.__setattr__(self, "_rec", rec)
        object.__setattr__(self, "_name", name)
        object.__setattr__(self, "_inc", inc)

    def __getattr__(self, meth):
        rec, name, inc = self._rec, self._name, self._inc

        def call(*args, **kwargs):
            rec.emit(name, lambda e: getattr(e, meth)(*args, **kwargs), inc)

        return call

    def group(self, fn, pre=(), post=(), tick=True, evflag=False, tkwait=True):
        self._rec.emit(self._name, fn, self._inc if tick else 0, pre, post, evflag, tkwait)

    def waitsem(self, sem, val):
        self._rec.emit(self._name, lambda e: e.wait_ge(sem, val), 0)


class Stream:

    def __init__(self, rec, nc, stack, name, qname, nbuf=2):
        self.rec = rec
        self.qname = qname
        self.nbuf = nbuf
        self.load_sems = [stack.enter_context(nc.semaphore(f"{name}_ld{i}"))
                          for i in range(nbuf)]
        self.free_sem = stack.enter_context(nc.semaphore(f"{name}_fr"))
        self.n = 0

    def load(self, out_ap, in_ap):
        slot = self.n % self.nbuf
        need_free = self.n - self.nbuf + 1
        ls, fs = self.load_sems[slot], self.free_sem

        def fn(e, _nf=need_free, _o=out_ap, _i=in_ap):
            if _nf > 0:
                e.wait_ge(fs, _nf)
            return e.dma_start(out=_o, in_=_i)

        self.rec.emit(self.qname, fn, 0, (), ((ls, 16),))
        self.n += 1
        nuse = (self.n - 1) // self.nbuf + 1
        return slot, (ls, 16 * nuse), (fs, 1)


def build_program(cfg):
    T = cfg["T"]
    NSTRIP = cfg["n_strips"]
    NOWN = cfg["own_strips"]
    NPRE = NSTRIP - NOWN
    SEQ = NSTRIP * T
    NCH = T // 128
    OWN_TOK = NOWN * T
    NKH = DFF // 128

    nc = bass.Bass("TRN2", target_bir_lowering=False)

    xin = nc.dram_tensor("xin", [SEQ, D], F32, kind="ExternalInput")
    wq_e = nc.dram_tensor("Wq", [D, D], F32, kind="ExternalInput")
    wk_e = nc.dram_tensor("Wk", [D, D], F32, kind="ExternalInput")
    wv_e = nc.dram_tensor("Wv", [D, D], F32, kind="ExternalInput")
    wg_e = nc.dram_tensor("Wg", [D, D], F32, kind="ExternalInput")
    wo_e = nc.dram_tensor("Wout", [D, D], F32, kind="ExternalInput")
    wgk1_e = nc.dram_tensor("Wgk1", [D, 16], F32, kind="ExternalInput")
    wgk2_e = nc.dram_tensor("Wgk2", [16, D], F32, kind="ExternalInput")
    ffn_e = {nm: nc.dram_tensor(nm, shape, F32, kind="ExternalInput")
             for nm, shape in (("w1", [DFF, D]), ("w2", [DFF, D]), ("h1", [DFF, DFF]),
                               ("h2", [DFF, DFF]), ("w3", [D, DFF]))}
    scales_e = nc.dram_tensor("scales", [128, 136], F32, kind="ExternalInput")
    bgk_e = nc.dram_tensor("bgk_bc", [128, D], F32, kind="ExternalInput")
    gam_e = nc.dram_tensor("gam_bc", [128, D], F32, kind="ExternalInput")
    bet_e = nc.dram_tensor("bet_bc", [128, D], F32, kind="ExternalInput")
    tri16_e = nc.dram_tensor("tri16_c", [128, 128], BF16, kind="ExternalInput")
    trirep_e = nc.dram_tensor("trirep_c", [128, 512], BF16, kind="ExternalInput")
    sixcol_e = nc.dram_tensor("sixcol_c", [128, 1], BF16, kind="ExternalInput")
    onesrow_e = nc.dram_tensor("onesrow_c", [1, 128], BF16, kind="ExternalInput")
    ident_e = nc.dram_tensor("ident_c", [128, 128], F32, kind="ExternalInput")
    out_ext = nc.dram_tensor("out", [OWN_TOK, D], F32, kind="ExternalOutput")

    wn = {nm: nc.dram_tensor(f"wn_{nm}", list(ffn_e[nm].shape), BF16) for nm in ffn_e}
    wt = {"w1": nc.dram_tensor("wt_w1", [2, 8, 128, 4, 512], BF16),
          "w2": nc.dram_tensor("wt_w2", [2, 8, 128, 4, 512], BF16),
          "h1": nc.dram_tensor("wt_h1", [8, 8, 128, 4, 512], BF16),
          "h2": nc.dram_tensor("wt_h2", [8, 8, 128, 4, 512], BF16),
          "w3": nc.dram_tensor("wt_w3", [16, 128, 2, 1024], BF16)}
    wa = {nm: nc.dram_tensor(f"wa_{nm}", [4, 128, 2, 1024], BF16)
          for nm in ("Wq", "Wk", "Wv", "Wg", "Wout")}
    r1sc = nc.dram_tensor("r1sc", [SEQ, D], F32)
    r2sc = nc.dram_tensor("r2sc", [OWN_TOK, D], F32)

    with ExitStack() as stack:
        E = stack.enter_context

        def sb(name, shape, dt):
            return E(nc.sbuf_tensor(name, shape, dt))

        h_a = sb("h_a", [128, NKH * T], BF16)
        h_b = sb("h_b", [128, NKH * T], BF16)
        x_sb = sb("x_sb", [128, 2 * D], F32)
        xn_fm = sb("xn_fm", [128, 8 * T], BF16)
        xn2_fm = sb("xn2_fm", [128, 8 * T], BF16)
        xnt = sb("xnt", [128, D], F32)
        r1c = sb("r1c", [128, D], F32)
        wsl = sb("wsl", [128, 5 * 2048], BF16)
        w3sl = sb("w3sl", [128, 2 * 2 * D], BF16)
        asl = sb("asl", [128, 2 * 2 * D], BF16)
        agk1_sb = sb("agk1_sb", [128, 8 * 16], BF16)
        agk2_sb = sb("agk2_sb", [16, D], BF16)
        scales = sb("scales_sb", [128, 136], F32)
        tri16 = sb("tri16", [128, 128], BF16)
        trirep = sb("trirep", [128, 512], BF16)
        sixcol = sb("sixcol", [128, 1], BF16)
        onesrow = sb("onesrow", [1, 128], BF16)
        ident = sb("ident", [128, 128], F32)
        bgk_bc = sb("bgk_bc_s", [128, D], F32)
        gam_bc = sb("gam_bc_s", [128, D], F32)
        bet_bc = sb("bet_bc_s", [128, D], F32)
        S = sb("S", [64, D], F32)
        S0x_bf = sb("S0x_bf", [64, D], BF16)
        zb = sb("zb", [128, D], F32)
        t1 = sb("t1", [128, D], F32)
        t2 = sb("t2", [128, D], F32)
        cumG = sb("cumG", [128, D], F32)
        g_bf = sb("g_bf", [128, D], BF16)
        vt_bf = sb("vt_bf", [128, D], BF16)
        ktok = sb("ktok", [128, D], BF16)
        ktf = sb("ktf", [128, D], F32)
        kfm = sb("kfm", [64, 2 * D], BF16)
        qfm = sb("qfm", [64, 2 * D], BF16)
        am = sb("am", [128, 512], BF16)
        z1_bf = sb("z1_bf", [16, 128], BF16)
        mid_bf = sb("mid_bf", [1, D], BF16)
        emu_bf = sb("emu_bf", [1, D], BF16)
        stat = sb("stat", [128, 12], F32)
        mv = sb("mv", [128, 2], F32)
        rstd = sb("rstd", [128, 1], F32)
        nmean = sb("nmean", [128, 1], F32)
        stg = wsl
        stgf = t1

        out_pre = zb
        r1d = zb
        lam = t1
        dinv = t2
        cumGc = cumG
        S0x = cumG
        f_bf = mid_bf
        yfm = g_bf
        qtf = ktf
        ytf = ktf
        w3tmp = ktf

        ps4b = E(nc.psum_tensor("ps4b", [128, 2048], F32))
        psD = E(nc.psum_tensor("psD", [128, 1024], F32))
        psS = E(nc.psum_tensor("psS", [128, 1024], F32))

        tk_sem = E(nc.semaphore("tk"))
        ev_sem = E(nc.semaphore("ev"))
        xld_sem = E(nc.semaphore("xld"))
        r1st_sem = E(nc.semaphore("r1st"))
        qsems = {q: E(nc.semaphore(f"q_{q}")) for q in ("sync", "gp", "ac")}
        block = E(nc.Block())

        rec = Rec()
        xld_n = [0]
        sync = EngRec(rec, "sync", 16)
        gp = EngRec(rec, "gp", 16)
        pe = EngRec(rec, "pe", 1)
        dv = EngRec(rec, "dv", 1)
        ac = EngRec(rec, "ac", 1)

        wstr = Stream(rec, nc, stack, "wst", "sync", nbuf=5)
        w3str = Stream(rec, nc, stack, "w3st", "ac", nbuf=2)
        astr = Stream(rec, nc, stack, "ast", "gp", nbuf=2)

        def evp(engrec, fn, pre=(), tkwait=False):
            engrec.group(fn, pre=pre, post=((ev_sem, 1),), tick=False, evflag=True,
                         tkwait=tkwait)

        sync.dma_start(out=scales[:, :], in_=scales_e.ap())

        def cast_store(ext, scratch):
            for j in range(8):
                view = stg[:, 0:D]
                gp.dma_start(out=view, in_=ext.ap()[j * 128:(j + 1) * 128, :])
                sync.dma_start(out=scratch.ap()[j // 2, :, j % 2, :], in_=view)

        def cast_store_scaled(ext, scratch, scol0, extra=1.0):
            rows, cols = ext.shape
            nch = rows // 128
            for j in range(nch):
                for h0 in range(0, cols, 1024):
                    w = min(1024, cols - h0)
                    gp.dma_start(out=stgf[:, 0:w],
                                 in_=ext.ap()[j * 128:(j + 1) * 128, h0:h0 + w])
                    dv.tensor_scalar(stg[:, 0:w], stgf[:, 0:w],
                                     scales[:, scol0 + j:scol0 + j + 1], extra,
                                     ALU.mult, ALU.mult)
                    sync.dma_start(out=scratch.ap()[j * 128:(j + 1) * 128, h0:h0 + w],
                                   in_=stg[:, 0:w])

        for ext, nm in ((wq_e, "Wq"), (wk_e, "Wk"), (wv_e, "Wv"), (wg_e, "Wg"), (wo_e, "Wout")):
            cast_store(ext, wa[nm])
        gp.dma_start(out=agk1_sb[:, :].rearrange("p (j o) -> p j o", j=8),
                     in_=wgk1_e.ap().rearrange("(j p) o -> p j o", p=128))
        gp.dma_start(out=agk2_sb[:, :], in_=wgk2_e.ap())
        cast_store_scaled(ffn_e["w1"], wn["w1"], 0)
        cast_store_scaled(ffn_e["w2"], wn["w2"], 32)
        cast_store_scaled(ffn_e["h1"], wn["h1"], 64)
        cast_store_scaled(ffn_e["h2"], wn["h2"], 96)
        cast_store_scaled(ffn_e["w3"], wn["w3"], 128, extra=0.5)

        def transpose_weight(nm):
            O, I = wn[nm].shape
            for j in range(I // 128):
                for o0 in range(0, O, 1024):
                    w = min(1024, O - o0)
                    sync.dma_start(out=stg[:, 0:w],
                                   in_=wn[nm].ap()[o0:o0 + w, j * 128:(j + 1) * 128],
                                   transpose=True)
                    if nm == "w3":
                        for h0 in range(0, w, 1024):
                            sync.dma_start(out=wt[nm].ap()[j // 2, :, j % 2,
                                                           0:1024] if w == 1024 else
                                           wt[nm].ap()[j // 2, :, j % 2, :],
                                           in_=stg[:, h0:h0 + 1024])
                    else:
                        for gg in range(w // 512):
                            g = (o0 // 512) + gg
                            sync.dma_start(out=wt[nm].ap()[j // 4, g, :, j % 4, :],
                                           in_=stg[:, gg * 512:(gg + 1) * 512])

        for nm in ("w1", "w2", "h1", "h2", "w3"):
            transpose_weight(nm)

        for dst, src in ((tri16, tri16_e), (trirep, trirep_e),
                         (sixcol, sixcol_e), (onesrow, onesrow_e), (ident, ident_e),
                         (bgk_bc, bgk_e), (gam_bc, gam_e), (bet_bc, bet_e)):
            sync.dma_start(out=dst[:, :], in_=src.ap())
        dv.memset(S[:, :], 0.0)

        def ln_to(dst_ap, src_ap, ev_guard=False):
            def f(e):
                e.bn_stats(stat[:, 0:6], src_ap[:, 0:512])
                return e.bn_stats(stat[:, 6:12], src_ap[:, 512:1024])
            if ev_guard:
                dv.waitsem(ev_sem, rec.ev)
            dv.group(f)
            dv.bn_aggr(mv[:, :], stat[:, :])
            dv.tensor_scalar_add(rstd[:, :], mv[:, 1:2], EPS)
            ac.activation(rstd[:, :], rstd[:, :], AF.Ln)
            ac.activation(rstd[:, :], rstd[:, :], AF.Exp, scale=-0.5)
            dv.scalar_tensor_tensor(nmean[:, :], mv[:, 0:1], -1.0, rstd[:, :], ALU.mult, ALU.mult)
            dv.tensor_scalar(dst_ap, src_ap, rstd[:, 0:1], nmean[:, 0:1], ALU.mult, ALU.add)

        def transpose_chunk(fm_sb, tok_f32, cc):
            def f(e):
                last = None
                for j in range(8):
                    last = e.transpose(psS[:, j * 128:(j + 1) * 128],
                                       tok_f32[:, j * 128:(j + 1) * 128], ident[:, :])
                return last
            pe.group(f)
            dv.tensor_copy(
                fm_sb[:, :].rearrange("p (j t) -> p j t", j=8)[:, :, cc * 128:(cc + 1) * 128],
                psS[:, :].rearrange("p (j c) -> p j c", j=8))

        def aproj(fm_sb, cc, wext, ps):
            for j2 in range(4):
                j0 = 2 * j2
                slot, pre, post = astr.load(
                    asl[:, (astr.n % 2) * 2 * D:(astr.n % 2 + 1) * 2 * D],
                    wext.ap()[j2, :, :, :].rearrange("p kk c -> p (kk c)"))

                def f(e, _sl=slot, _j0=j0):
                    last = None
                    for kk in range(2):
                        j = _j0 + kk
                        base = _sl * 2 * D + kk * D
                        lhsT = fm_sb[:, j * T + cc * 128: j * T + (cc + 1) * 128]
                        e.matmul(ps[:, 0:512], lhsT, asl[:, base:base + 512],
                                 start=(j == 0), stop=(j == 7))
                        last = e.matmul(ps[:, 512:1024], lhsT, asl[:, base + 512:base + 1024],
                                        start=(j == 0), stop=(j == 7))
                    return last
                pe.group(f, pre=(pre,), post=(post,))

        def wout_mm(cc, ps):
            for j2 in range(4):
                j0 = 2 * j2
                slot, pre, post = astr.load(
                    asl[:, (astr.n % 2) * 2 * D:(astr.n % 2 + 1) * 2 * D],
                    wa["Wout"].ap()[j2, :, :, :].rearrange("p kk c -> p (kk c)"))

                def f(e, _sl=slot, _j0=j0):
                    last = None
                    for kk in range(2):
                        j = _j0 + kk
                        base = _sl * 2 * D + kk * D
                        lhsT = yfm[:, j * 128:(j + 1) * 128]
                        e.matmul(ps[:, 0:512], lhsT, asl[:, base:base + 512],
                                 start=(j == 0), stop=(j == 7))
                        last = e.matmul(ps[:, 512:1024], lhsT, asl[:, base + 512:base + 1024],
                                        start=(j == 0), stop=(j == 7))
                    return last
                pe.group(f, pre=(pre,), post=(post,))

        def mm2S(lhsT, rhs0, rhs1, rows=(0, 128)):
            def f(e):
                e.matmul(psS[rows[0]:rows[1], 0:512], lhsT, rhs0, start=True, stop=True)
                return e.matmul(psS[rows[0]:rows[1], 512:1024], lhsT, rhs1, start=True, stop=True)
            pe.group(f)

        FLUSH_PER_GROUP = 6

        def stream_layer(wtname, nI, rhs_sb, evac):
            wte = wt[wtname]
            for g in range(wte.shape[1]):
                for k4 in range(nI // 4):
                    k0 = 4 * k4
                    slot, pre, post = wstr.load(
                        wsl[:, (wstr.n % 5) * 2048:(wstr.n % 5 + 1) * 2048],
                        wte.ap()[k4, g, :, :, :].rearrange("p kk c -> p (kk c)"))
                    evneed = rec.ev

                    def f(e, _sl=slot, _k0=k0, _nI=nI, _rhs=rhs_sb, _ev=evneed):
                        if _ev and _k0 == 0:
                            e.wait_ge(ev_sem, _ev)
                        last = None
                        for kk in range(4):
                            k = _k0 + kk
                            for u in range(4):
                                last = e.matmul(
                                    ps4b[:, u * 512:(u + 1) * 512],
                                    wsl[:, _sl * 2048 + kk * 512 + u * 128:
                                        _sl * 2048 + kk * 512 + (u + 1) * 128],
                                    _rhs[:, k * T:(k + 1) * T],
                                    start=(k == 0), stop=(k == _nI - 1))
                        return last
                    pe.group(f, pre=(pre,), post=(post,), tick=False)
                    rec.flush(2)
                evac(g)
                rec.flush(FLUSH_PER_GROUP)

        def ffn_pass(src_fm, out_cb):
            def ev_w1(g):
                evp(ac, lambda e, _g=g: e.activation(
                    h_a[:, 4 * _g * T:(4 * _g + 4) * T], ps4b[:, :], AF.Silu),
                    pre=((wstr.free_sem, wstr.n),))
            stream_layer("w1", 8, src_fm, ev_w1)

            def ev_w2(g):
                evp(dv, lambda e, _g=g: e.tensor_mul(
                    h_b[:, 4 * _g * T:(4 * _g + 4) * T], ps4b[:, :],
                    h_a[:, 4 * _g * T:(4 * _g + 4) * T]),
                    pre=((wstr.free_sem, wstr.n), (ev_sem, rec.ev),
                         (w3str.free_sem, w3str.n)))
            stream_layer("w2", 8, src_fm, ev_w2)

            def ev_h1(g):
                evp(ac, lambda e, _g=g: e.activation(
                    h_a[:, 4 * _g * T:(4 * _g + 4) * T], ps4b[:, :], AF.Gelu),
                    pre=((wstr.free_sem, wstr.n),))
            stream_layer("h1", NKH, h_b, ev_h1)

            def ev_h2(g):
                evp(ac, lambda e, _g=g: e.activation(
                    h_b[:, 4 * _g * T:(4 * _g + 4) * T], ps4b[:, :], AF.Gelu),
                    pre=((wstr.free_sem, wstr.n), (w3str.free_sem, w3str.n)))
            stream_layer("h2", NKH, h_a, ev_h2)

            for cc in range(NCH):
                for m2 in range(NKH // 2):
                    m0 = 2 * m2
                    slot, pre, post = w3str.load(
                        w3sl[:, (w3str.n % 2) * 2 * D:(w3str.n % 2 + 1) * 2 * D],
                        wt["w3"].ap()[m2, :, :, :].rearrange("p kk c -> p (kk c)"))
                    evneed = rec.ev

                    def f(e, _sl=slot, _m0=m0, _cc=cc, _ev=evneed):
                        if _ev and _m0 == 0:
                            e.wait_ge(ev_sem, _ev)
                        last = None
                        for kk in range(2):
                            m = _m0 + kk
                            base = _sl * 2 * D + kk * D
                            lhsT = h_b[:, m * T + _cc * 128: m * T + (_cc + 1) * 128]
                            e.matmul(psD[:, 0:512], lhsT, w3sl[:, base:base + 512],
                                     start=(m == 0), stop=(m == NKH - 1))
                            last = e.matmul(psD[:, 512:1024], lhsT, w3sl[:, base + 512:base + 1024],
                                            start=(m == 0), stop=(m == NKH - 1))
                        return last
                    pe.group(f, pre=(pre,), post=(post,), tick=False)
                out_cb(cc)
                rec.flush(2 * FLUSH_PER_GROUP)

        def scan_chunk(cc, own, strip):
            tok0 = strip * T + cc * 128
            gp.dma_start(out=r1d[:, :], in_=r1sc.ap()[tok0:tok0 + 128, :])
            ln_to(xnt[:, :], r1d[:, :])
            transpose_chunk(xn2_fm, xnt, cc)

            def f_z1(e):
                last = None
                for j in range(8):
                    last = e.matmul(psS[0:16, 0:128], agk1_sb[:, j * 16:(j + 1) * 16],
                                    xn2_fm[:, j * T + cc * 128: j * T + (cc + 1) * 128],
                                    start=(j == 0), stop=(j == 7))
                return last
            pe.group(f_z1)
            dv.tensor_copy(z1_bf[:, :], psS[0:16, 0:128])
            mm2S(z1_bf[:, :], agk2_sb[:, 0:512], agk2_sb[:, 512:1024])
            dv.tensor_add(zb[:, :], psS[:, :], bgk_bc[:, :])
            ac.activation(t1[:, :], zb[:, :], AF.Abs)
            ac.activation(t2[:, :], t1[:, :], AF.Exp, scale=-1.0)
            dv.tensor_scalar_add(t1[:, :], t2[:, :], 1.0)
            ac.activation(t2[:, :], t1[:, :], AF.Ln)
            dv.tensor_scalar_min(t1[:, :], zb[:, :], 0.0)
            dv.tensor_sub(g_bf[:, :], t1[:, :], t2[:, :])

            mm2S(tri16[:, :], g_bf[:, 0:512], g_bf[:, 512:1024])
            dv.tensor_copy(cumG[:, :], psS[:, :])
            dv.tensor_copy(mid_bf[:, :], cumG[64:65, :])

            def f_sumg(e):
                e.matmul(psS[0:1, 0:512], sixcol[:, :], g_bf[:, 0:512], start=True, stop=True)
                return e.matmul(psS[0:1, 512:1024], sixcol[:, :], g_bf[:, 512:1024],
                                start=True, stop=True)
            pe.group(f_sumg)
            dv.tensor_sub(xnt[0:1, :], psS[0:1, :], mid_bf[:, :])
            mm2S(onesrow[0:1, :], mid_bf[0:1, 0:512], mid_bf[0:1, 512:1024])
            dv.tensor_sub(cumGc[:, :], cumG[:, :], psS[:, :])
            ac.activation(lam[:, :], cumGc[:, :], AF.Exp)
            ac.activation(dinv[:, :], cumGc[:, :], AF.Exp, scale=-1.0)
            ac.activation(emu_bf[:, :], mid_bf[:, :], AF.Exp)
            ac.activation(f_bf[:, :], xnt[0:1, :], AF.Exp)

            mm2S(onesrow[0:1, 0:64], emu_bf[0:1, 0:512], emu_bf[0:1, 512:1024], rows=(0, 64))
            dv.tensor_mul(S0x[0:64, :], S[:, :], psS[0:64, :])
            dv.tensor_copy(S0x_bf[:, :], S0x[0:64, :])

            aproj(xn2_fm, cc, wa["Wv"], psS)
            dv.tensor_mul(vt_bf[:, :], psS[:, :], dinv[:, :])
            aproj(xn2_fm, cc, wa["Wk"], psS)
            dv.tensor_copy(ktok[:, :], psS[:, :])
            if own:
                dv.tensor_copy(ktf[:, :], psS[:, :])

            def f_ds(e):
                last = None
                for h in range(NH):
                    last = e.matmul(psS[0:64, h * 64:(h + 1) * 64],
                                    ktok[:, h * 64:(h + 1) * 64], vt_bf[:, h * 64:(h + 1) * 64],
                                    start=(h % 8 == 0), stop=(h % 8 == 7))
                return last
            pe.group(f_ds)
            dv.tensor_add(S[:, :], S0x[0:64, :], psS[0:64, :])
            mm2S(onesrow[0:1, 0:64], f_bf[0:1, 0:512], f_bf[0:1, 512:1024], rows=(0, 64))
            dv.tensor_mul(S[:, :], S[:, :], psS[0:64, :])

            if not own:
                return

            def tr8(e, src_sb):
                last = None
                for j in range(8):
                    last = e.transpose(psS[:, j * 128:(j + 1) * 128],
                                       src_sb[:, j * 128:(j + 1) * 128], ident[:, :])
                return last
            pe.group(lambda e: tr8(e, ktf))
            dv.tensor_copy(kfm[0:64, 0:1024].rearrange("p (j c) -> p j c", j=8),
                           psS[0:64, :].rearrange("p (j c) -> p j c", j=8))
            dv.tensor_copy(kfm[0:64, 1024:2048].rearrange("p (j c) -> p j c", j=8),
                           psS[64:128, :].rearrange("p (j c) -> p j c", j=8))
            aproj(xn2_fm, cc, wa["Wq"], psS)
            dv.tensor_copy(qtf[:, :], psS[:, :])
            pe.group(lambda e: tr8(e, qtf))
            dv.tensor_copy(qfm[0:64, 0:1024].rearrange("p (j c) -> p j c", j=8),
                           psS[0:64, :].rearrange("p (j c) -> p j c", j=8))
            dv.tensor_copy(qfm[0:64, 1024:2048].rearrange("p (j c) -> p j c", j=8),
                           psS[64:128, :].rearrange("p (j c) -> p j c", j=8))

            for q4 in range(4):
                def f_a(e, _q=q4):
                    last = None
                    for hh in range(4):
                        h = _q * 4 + hh
                        b = (h >> 1) + 8 * (h & 1)
                        last = e.matmul(psS[:, hh * 128:(hh + 1) * 128],
                                        kfm[0:64, b * 128:(b + 1) * 128],
                                        qfm[0:64, b * 128:(b + 1) * 128],
                                        start=(hh == 0), stop=(hh == 3))
                    return last
                pe.group(f_a)
                dv.tensor_mul(am[:, :], psS[:, 0:512], trirep[:, :])

                def f_oi(e, _q=q4):
                    last = None
                    for hh in range(4):
                        h = _q * 4 + hh
                        b = (h >> 1) + 8 * (h & 1)
                        col = 512 + 64 * hh
                        e.matmul(psS[:, col:col + 64], am[:, hh * 128:(hh + 1) * 128],
                                 vt_bf[:, h * 64:(h + 1) * 64],
                                 start=(hh == 0), stop=False)
                        last = e.matmul(psS[:, col:col + 64],
                                        qfm[0:64, b * 128:(b + 1) * 128],
                                        S0x_bf[0:64, h * 64:(h + 1) * 64],
                                        start=False, stop=(hh == 3))
                    return last
                pe.group(f_oi)
                dv.tensor_mul(out_pre[:, q4 * 256:(q4 + 1) * 256], psS[:, 512:768],
                              lam[:, q4 * 256:(q4 + 1) * 256])

            ln_to(t1[:, :], out_pre[:, :])
            dv.tensor_mul(t2[:, :], t1[:, :], gam_bc[:, :])
            dv.tensor_add(t2[:, :], t2[:, :], bet_bc[:, :])
            aproj(xn2_fm, cc, wa["Wg"], psS)
            ac.activation(t1[:, :], psS[:, :], AF.Silu)
            dv.tensor_mul(ytf[:, :], t2[:, :], t1[:, :])
            pe.group(lambda e: tr8(e, ytf))
            dv.tensor_copy(yfm[:, :].rearrange("p (j c) -> p j c", j=8),
                           psS[:, :].rearrange("p (j c) -> p j c", j=8))
            wout_mm(cc, psS)
            own_tok0 = (strip - NPRE) * T + cc * 128
            gp.dma_start(out=r1d[:, :], in_=r1sc.ap()[tok0:tok0 + 128, :])
            dv.tensor_add(xnt[:, :], r1d[:, :], psS[:, :])
            gp.dma_start(out=r2sc.ap()[own_tok0:own_tok0 + 128, :], in_=xnt[:, :])

        def load_x(s, cc):
            tok0 = s * T + cc * 128
            slot = cc % 2
            gp.group(lambda e, _t=tok0, _sl=slot: e.dma_start(
                out=x_sb[:, _sl * D:(_sl + 1) * D], in_=xin.ap()[_t:_t + 128, :]),
                pre=((ev_sem, rec.ev),), post=((xld_sem, 16),))
            xld_n[0] += 1
            return xld_n[0]

        def record_ffn1(s):
            for cc in range(NCH):
                n = load_x(s, cc)
                dv.waitsem(xld_sem, 16 * n)
                ln_to(xnt[:, :], x_sb[:, (cc % 2) * D:(cc % 2 + 1) * D],
                      ev_guard=(cc == 0))
                transpose_chunk(xn_fm, xnt, cc)

            def out1(cc, _s=s):
                n = load_x(_s, cc)
                evp(dv, lambda e, _cc=cc: e.tensor_add(
                    r1c[:, :], psD[:, :], x_sb[:, (_cc % 2) * D:(_cc % 2 + 1) * D]),
                    pre=((w3str.free_sem, w3str.n), (xld_sem, 16 * n),
                         (r1st_sem, 16 * (_s * NCH + cc))))
                tok0 = _s * T + cc * 128
                gp.group(lambda e, _t=tok0: e.dma_start(
                    out=r1sc.ap()[_t:_t + 128, :], in_=r1c[:, :]),
                    pre=((ev_sem, rec.ev),), post=((r1st_sem, 16),))

            ffn_pass(xn_fm, out1)

        def record_ffn2(s):
            for cc in range(NCH):
                own_tok0 = (s - NPRE) * T + cc * 128
                gp.group(lambda e, _t=own_tok0: e.dma_start(
                    out=r1c[:, :], in_=r2sc.ap()[_t:_t + 128, :]),
                    pre=((ev_sem, rec.ev),))
                ln_to(xnt[:, :], r1c[:, :], ev_guard=(cc == 0))
                transpose_chunk(xn_fm, xnt, cc)

            def out2(cc, _s=s):
                dv.waitsem(w3str.free_sem, w3str.n)
                evp(dv, lambda e: e.tensor_copy(w3tmp[:, :], psD[:, :]), tkwait=True)
                own_tok0 = (_s - NPRE) * T + cc * 128
                gp.group(lambda e, _t=own_tok0: e.dma_start(
                    out=r1c[:, :], in_=r2sc.ap()[_t:_t + 128, :]),
                    pre=((ev_sem, rec.ev),))
                evp(dv, lambda e: e.tensor_add(xnt[:, :], r1c[:, :], w3tmp[:, :]),
                    pre=((ev_sem, rec.ev),), tkwait=True)
                gp.group(lambda e, _t=own_tok0: e.dma_start(
                    out=out_ext.ap()[_t:_t + 128, :], in_=xnt[:, :]),
                    pre=((ev_sem, rec.ev),))

            ffn_pass(xn_fm, out2)

        prev_own = False
        for s in range(NSTRIP):
            own = s >= NPRE
            record_ffn1(s)
            rec.flush_all()
            if prev_own:
                record_ffn2(s - 1)
            rec.defer = True
            for cc in range(NCH):
                scan_chunk(cc, own, s)
            rec.defer = False
            prev_own = own
        rec.flush_all()
        record_ffn2(NSTRIP - 1)
        rec.flush_all()

        final_ticket = rec.val
        final_ev = rec.ev

        def replay(engname):
            skip_self = engname == "pe"

            def f(eng):
                low = 0
                qcnt = 0
                for (w, fn, inc, pre, post) in rec.items[engname]:
                    if w is not None and w > low:
                        eng.wait_ge(tk_sem, w)
                        low = w
                    for (sem, val) in pre:
                        eng.wait_ge(sem, val)
                    inst = fn(eng)
                    if inc == 16:
                        qs = qsems[engname]
                        assert inst is not None
                        inst.then_inc(qs, 16)
                        qcnt += 16
                        eng.wait_ge(qs, qcnt)
                        eng.nop().then_inc(tk_sem, 16)
                        for (sem, val) in post:
                            eng.nop().then_inc(sem, val)
                    elif inc or post:
                        assert inst is not None, f"group must return an instruction ({engname})"
                        updates = ([(tk_sem, inc)] if inc else []) + list(post)
                        inst.then_inc(*updates[0])
                        for (sem, val) in updates[1:]:
                            eng.nop().then_inc(sem, val)
                    if skip_self:
                        low += inc
                if engname == "gp":
                    eng.wait_ge(tk_sem, final_ticket)
                    eng.wait_ge(ev_sem, final_ev)
            return f

        block.sync(replay("sync"))
        block.gpsimd(replay("gp"))
        block.tensor(replay("pe"))
        block.vector(replay("dv"))
        block.scalar(replay("ac"))
        for k in list(rec.items):
            rec.items[k] = []
    return nc


def _consts(cfg):
    import ml_dtypes
    tri = np.tril(np.ones((128, 128), dtype=np.float32)).T
    bf = lambda a: a.astype(ml_dtypes.bfloat16)
    return dict(
        tri16_c=bf(tri / 16.0),
        trirep_c=bf(np.tile(tri, (1, 4))),
        sixcol_c=bf(np.full((128, 1), 1 / 16.0, dtype=np.float32)),
        onesrow_c=bf(np.ones((1, 128), dtype=np.float32)),
        ident_c=np.eye(128, dtype=np.float32),
    )


def make_in_maps(cfg, x_full, W):
    T, NSTRIP, NOWN = cfg["T"], cfg["n_strips"], cfg["own_strips"]
    SEQ = NSTRIP * T
    SLAB = NOWN * T
    consts = _consts(cfg)
    f32 = np.float32

    def fm_scales(s):
        return np.ascontiguousarray(s.reshape(-1, 128).T.astype(f32))

    scales = np.concatenate([fm_scales(W["ffn_w1_s"]), fm_scales(W["ffn_w2_s"]),
                             fm_scales(W["ffn_h1_s"]), fm_scales(W["ffn_h2_s"]),
                             fm_scales(W["ffn_w3_s"])], axis=1)
    bcast = lambda v: np.ascontiguousarray(np.broadcast_to(v.astype(f32), (128, D)))
    shared = dict(
        Wq=W["attn_Wq"], Wk=W["attn_Wk"], Wv=W["attn_Wv"], Wg=W["attn_Wg"],
        Wout=W["attn_Wout"], Wgk1=W["attn_Wgk1"], Wgk2=W["attn_Wgk2"],
        w1=W["ffn_w1_w"], w2=W["ffn_w2_w"], h1=W["ffn_h1_w"], h2=W["ffn_h2_w"],
        w3=W["ffn_w3_w"],
        scales=scales,
        bgk_bc=bcast(W["attn_bgk2"]),
        gam_bc=bcast(W["attn_gamma"]),
        bet_bc=bcast(W["attn_beta"]),
        **consts,
    )
    in_maps = []
    for core in range(8):
        b, p = divmod(core, 4)
        take = (p + 1) * SLAB
        xi = np.zeros((SEQ, D), dtype=f32)
        xi[SEQ - take:] = x_full[b, :take]
        m = dict(shared)
        m["xin"] = xi
        in_maps.append(m)
    return in_maps


def kernel(**inputs):
    cfg = CFG_FULL
    W = {
        "attn_Wq": np.asarray(inputs["L0_attn_Wq"]),
        "attn_Wk": np.asarray(inputs["L0_attn_Wk"]),
        "attn_Wv": np.asarray(inputs["L0_attn_Wv"]),
        "attn_Wg": np.asarray(inputs["L0_attn_Wg"]),
        "attn_Wout": np.asarray(inputs["L0_attn_Wout"]),
        "attn_Wgk1": np.asarray(inputs["L0_attn_Wgk1"]),
        "attn_Wgk2": np.asarray(inputs["L0_attn_Wgk2"]),
        "attn_bgk2": np.asarray(inputs["L0_attn_bgk2"]),
        "attn_gamma": np.asarray(inputs["L0_attn_gamma"]),
        "attn_beta": np.asarray(inputs["L0_attn_beta"]),
        "ffn_w1_w": np.asarray(inputs["L0_ffn_w1_w"]),
        "ffn_w1_s": np.asarray(inputs["L0_ffn_w1_s"]),
        "ffn_w2_w": np.asarray(inputs["L0_ffn_w2_w"]),
        "ffn_w2_s": np.asarray(inputs["L0_ffn_w2_s"]),
        "ffn_h1_w": np.asarray(inputs["L0_ffn_h1_w"]),
        "ffn_h1_s": np.asarray(inputs["L0_ffn_h1_s"]),
        "ffn_h2_w": np.asarray(inputs["L0_ffn_h2_w"]),
        "ffn_h2_s": np.asarray(inputs["L0_ffn_h2_s"]),
        "ffn_w3_w": np.asarray(inputs["L0_ffn_w3_w"]),
        "ffn_w3_s": np.asarray(inputs["L0_ffn_w3_s"]),
    }
    x_full = np.asarray(inputs["L0_attn_x"], dtype=np.float32)
    in_maps = make_in_maps(cfg, x_full, W)
    del x_full, W
    import gc
    gc.collect()
    nc = build_program(cfg)
    res = run_bass_kernel_spmd(nc, in_maps, core_ids=list(range(8)))
    SLAB = cfg["own_strips"] * cfg["T"]
    out = np.empty((2, 16384, D), dtype=np.float32)
    for core in range(8):
        b, p = divmod(core, 4)
        out[b, p * SLAB:(p + 1) * SLAB] = np.asarray(res.results[core]["out"])
    return out


# version 11
# speedup vs baseline: 1.0122x; 1.0122x over previous
import numpy as np
import concourse.bass as bass
import concourse.mybir as mybir
from concourse.bass_utils import run_bass_kernel_spmd
from contextlib import ExitStack

F32 = mybir.dt.float32
BF16 = mybir.dt.bfloat16
AF = mybir.ActivationFunctionType
ALU = mybir.AluOpType

D = 1024
DFF = 4096
NH, HD = 16, 64
EPS = 1e-5

CFG_FULL = dict(T=512, n_strips=32, own_strips=8)

ENGS = ("sync", "gp", "pe", "dv", "ac")


class Rec:
    def __init__(self):
        self.items = {k: [] for k in ENGS}
        self.val = 0
        self.ev = 0
        self.pending = []
        self.defer = False

    def emit(self, eng, fn, inc, pre=(), post=(), evflag=False, tkwait=True):
        if self.defer:
            self.pending.append((eng, fn, inc, tuple(pre), tuple(post), evflag, tkwait))
            return
        self._commit(eng, fn, inc, pre, post, evflag, tkwait)

    def _commit(self, eng, fn, inc, pre, post, evflag, tkwait):
        if evflag:
            self.ev += 1
        self.items[eng].append((self.val if tkwait else None, fn, inc,
                                tuple(pre), tuple(post)))
        self.val += inc

    def flush(self, k):
        n = 0
        while self.pending and n < k:
            (eng, fn, inc, pre, post, evflag, tkwait) = self.pending.pop(0)
            self._commit(eng, fn, inc, pre, post, evflag, tkwait)
            n += 1

    def flush_all(self):
        self.flush(len(self.pending))


class EngRec:
    def __init__(self, rec, name, inc):
        object.__setattr__(self, "_rec", rec)
        object.__setattr__(self, "_name", name)
        object.__setattr__(self, "_inc", inc)

    def __getattr__(self, meth):
        rec, name, inc = self._rec, self._name, self._inc

        def call(*args, **kwargs):
            rec.emit(name, lambda e: getattr(e, meth)(*args, **kwargs), inc)

        return call

    def group(self, fn, pre=(), post=(), tick=True, evflag=False, tkwait=True):
        self._rec.emit(self._name, fn, self._inc if tick else 0, pre, post, evflag, tkwait)

    def waitsem(self, sem, val):
        self._rec.emit(self._name, lambda e: e.wait_ge(sem, val), 0)


class Stream:

    def __init__(self, rec, nc, stack, name, qname, nbuf=2):
        self.rec = rec
        self.qname = qname
        self.nbuf = nbuf
        self.load_sems = [stack.enter_context(nc.semaphore(f"{name}_ld{i}"))
                          for i in range(nbuf)]
        self.free_sem = stack.enter_context(nc.semaphore(f"{name}_fr"))
        self.n = 0

    def load(self, out_ap, in_ap):
        slot = self.n % self.nbuf
        need_free = self.n - self.nbuf + 1
        ls, fs = self.load_sems[slot], self.free_sem

        def fn(e, _nf=need_free, _o=out_ap, _i=in_ap):
            if _nf > 0:
                e.wait_ge(fs, _nf)
            return e.dma_start(out=_o, in_=_i)

        self.rec.emit(self.qname, fn, 0, (), ((ls, 16),))
        self.n += 1
        nuse = (self.n - 1) // self.nbuf + 1
        return slot, (ls, 16 * nuse), (fs, 1)


def build_program(cfg):
    T = cfg["T"]
    NSTRIP = cfg["n_strips"]
    NOWN = cfg["own_strips"]
    NPRE = NSTRIP - NOWN
    SEQ = NSTRIP * T
    NCH = T // 128
    OWN_TOK = NOWN * T
    NKH = DFF // 128

    nc = bass.Bass("TRN2", target_bir_lowering=False)

    xin = nc.dram_tensor("xin", [SEQ, D], F32, kind="ExternalInput")
    wq_e = nc.dram_tensor("Wq", [D, D], F32, kind="ExternalInput")
    wk_e = nc.dram_tensor("Wk", [D, D], F32, kind="ExternalInput")
    wv_e = nc.dram_tensor("Wv", [D, D], F32, kind="ExternalInput")
    wg_e = nc.dram_tensor("Wg", [D, D], F32, kind="ExternalInput")
    wo_e = nc.dram_tensor("Wout", [D, D], F32, kind="ExternalInput")
    wgk1_e = nc.dram_tensor("Wgk1", [D, 16], F32, kind="ExternalInput")
    wgk2_e = nc.dram_tensor("Wgk2", [16, D], F32, kind="ExternalInput")
    ffn_e = {nm: nc.dram_tensor(nm, shape, F32, kind="ExternalInput")
             for nm, shape in (("w1", [DFF, D]), ("w2", [DFF, D]), ("h1", [DFF, DFF]),
                               ("h2", [DFF, DFF]), ("w3", [D, DFF]))}
    scales_e = nc.dram_tensor("scales", [128, 136], F32, kind="ExternalInput")
    bgk_e = nc.dram_tensor("bgk_bc", [128, D], F32, kind="ExternalInput")
    gam_e = nc.dram_tensor("gam_bc", [128, D], F32, kind="ExternalInput")
    bet_e = nc.dram_tensor("bet_bc", [128, D], F32, kind="ExternalInput")
    tri16_e = nc.dram_tensor("tri16_c", [128, 128], BF16, kind="ExternalInput")
    trirep_e = nc.dram_tensor("trirep_c", [128, 512], BF16, kind="ExternalInput")
    sixcol_e = nc.dram_tensor("sixcol_c", [128, 1], BF16, kind="ExternalInput")
    onesrow_e = nc.dram_tensor("onesrow_c", [1, 128], BF16, kind="ExternalInput")
    ident_e = nc.dram_tensor("ident_c", [128, 128], F32, kind="ExternalInput")
    out_ext = nc.dram_tensor("out", [OWN_TOK, D], F32, kind="ExternalOutput")

    wn = {nm: nc.dram_tensor(f"wn_{nm}", list(ffn_e[nm].shape), BF16) for nm in ffn_e}
    wt = {"w1": nc.dram_tensor("wt_w1", [2, 8, 128, 4, 512], BF16),
          "w2": nc.dram_tensor("wt_w2", [2, 8, 128, 4, 512], BF16),
          "h1": nc.dram_tensor("wt_h1", [8, 8, 128, 4, 512], BF16),
          "h2": nc.dram_tensor("wt_h2", [8, 8, 128, 4, 512], BF16),
          "w3": nc.dram_tensor("wt_w3", [16, 128, 2, 1024], BF16)}
    wa = {nm: nc.dram_tensor(f"wa_{nm}", [4, 128, 2, 1024], BF16)
          for nm in ("Wq", "Wk", "Wv", "Wg", "Wout")}
    r1sc = nc.dram_tensor("r1sc", [SEQ, D], F32)
    r2sc = nc.dram_tensor("r2sc", [OWN_TOK, D], F32)

    with ExitStack() as stack:
        E = stack.enter_context

        def sb(name, shape, dt):
            return E(nc.sbuf_tensor(name, shape, dt))

        h_a = sb("h_a", [128, NKH * T], BF16)
        h_b = sb("h_b", [128, NKH * T], BF16)
        x_sb = sb("x_sb", [128, 2 * D], F32)
        xn_fm = sb("xn_fm", [128, 8 * T], BF16)
        xn2_fm = sb("xn2_fm", [128, 8 * T], BF16)
        xnt = sb("xnt", [128, D], F32)
        r1c = sb("r1c", [128, D], F32)
        wsl = sb("wsl", [128, 5 * 2048], BF16)
        w3sl = sb("w3sl", [128, 2 * 2 * D], BF16)
        asl = sb("asl", [128, 2 * 2 * D], BF16)
        agk1_sb = sb("agk1_sb", [128, 8 * 16], BF16)
        agk2_sb = sb("agk2_sb", [16, D], BF16)
        scales = sb("scales_sb", [128, 136], F32)
        tri16 = sb("tri16", [128, 128], BF16)
        trirep = sb("trirep", [128, 512], BF16)
        sixcol = sb("sixcol", [128, 1], BF16)
        onesrow = sb("onesrow", [1, 128], BF16)
        ident = sb("ident", [128, 128], F32)
        bgk_bc = sb("bgk_bc_s", [128, D], F32)
        gam_bc = sb("gam_bc_s", [128, D], F32)
        bet_bc = sb("bet_bc_s", [128, D], F32)
        S = sb("S", [64, D], F32)
        S0x_bf = sb("S0x_bf", [64, D], BF16)
        zb = sb("zb", [128, D], F32)
        t1 = sb("t1", [128, D], F32)
        t2 = sb("t2", [128, D], F32)
        cumG = sb("cumG", [128, D], F32)
        g_bf = sb("g_bf", [128, D], BF16)
        vt_bf = sb("vt_bf", [128, D], BF16)
        ktok = sb("ktok", [128, D], BF16)
        ktf = sb("ktf", [128, D], F32)
        kfm = sb("kfm", [64, 2 * D], BF16)
        qfm = sb("qfm", [64, 2 * D], BF16)
        am = sb("am", [128, 512], BF16)
        z1_bf = sb("z1_bf", [16, 128], BF16)
        mid_bf = sb("mid_bf", [1, D], BF16)
        emu_bf = sb("emu_bf", [1, D], BF16)
        stat = sb("stat", [128, 12], F32)
        mv = sb("mv", [128, 2], F32)
        rstd = sb("rstd", [128, 1], F32)
        nmean = sb("nmean", [128, 1], F32)
        stg = wsl
        stgf = t1

        out_pre = zb
        r1d = zb
        lam = t1
        dinv = t2
        cumGc = cumG
        S0x = cumG
        f_bf = mid_bf
        yfm = g_bf
        qtf = ktf
        ytf = ktf
        w3tmp = ktf

        ps4b = E(nc.psum_tensor("ps4b", [128, 2048], F32))
        psD = E(nc.psum_tensor("psD", [128, 1024], F32))
        psS = E(nc.psum_tensor("psS", [128, 1024], F32))

        tk_sem = E(nc.semaphore("tk"))
        ev_sem = E(nc.semaphore("ev"))
        xld_sem = E(nc.semaphore("xld"))
        r1st_sem = E(nc.semaphore("r1st"))
        qsems = {q: E(nc.semaphore(f"q_{q}")) for q in ("sync", "gp", "ac")}
        block = E(nc.Block())

        rec = Rec()
        xld_n = [0]
        sync = EngRec(rec, "sync", 16)
        gp = EngRec(rec, "gp", 16)
        pe = EngRec(rec, "pe", 1)
        dv = EngRec(rec, "dv", 1)
        ac = EngRec(rec, "ac", 1)

        wstr = Stream(rec, nc, stack, "wst", "sync", nbuf=5)
        w3str = Stream(rec, nc, stack, "w3st", "ac", nbuf=2)
        astr = Stream(rec, nc, stack, "ast", "gp", nbuf=2)

        def evp(engrec, fn, pre=(), tkwait=False):
            engrec.group(fn, pre=pre, post=((ev_sem, 1),), tick=False, evflag=True,
                         tkwait=tkwait)

        sync.dma_start(out=scales[:, :], in_=scales_e.ap())

        def cast_store(ext, scratch):
            for j in range(8):
                view = stg[:, 0:D]
                gp.dma_start(out=view, in_=ext.ap()[j * 128:(j + 1) * 128, :])
                sync.dma_start(out=scratch.ap()[j // 2, :, j % 2, :], in_=view)

        def cast_store_scaled(ext, scratch, scol0, extra=1.0):
            rows, cols = ext.shape
            nch = rows // 128
            for j in range(nch):
                for h0 in range(0, cols, 1024):
                    w = min(1024, cols - h0)
                    gp.dma_start(out=stgf[:, 0:w],
                                 in_=ext.ap()[j * 128:(j + 1) * 128, h0:h0 + w])
                    dv.tensor_scalar(stg[:, 0:w], stgf[:, 0:w],
                                     scales[:, scol0 + j:scol0 + j + 1], extra,
                                     ALU.mult, ALU.mult)
                    sync.dma_start(out=scratch.ap()[j * 128:(j + 1) * 128, h0:h0 + w],
                                   in_=stg[:, 0:w])

        for ext, nm in ((wq_e, "Wq"), (wk_e, "Wk"), (wv_e, "Wv"), (wg_e, "Wg"), (wo_e, "Wout")):
            cast_store(ext, wa[nm])
        gp.dma_start(out=agk1_sb[:, :].rearrange("p (j o) -> p j o", j=8),
                     in_=wgk1_e.ap().rearrange("(j p) o -> p j o", p=128))
        gp.dma_start(out=agk2_sb[:, :], in_=wgk2_e.ap())
        cast_store_scaled(ffn_e["w1"], wn["w1"], 0)
        cast_store_scaled(ffn_e["w2"], wn["w2"], 32)
        cast_store_scaled(ffn_e["h1"], wn["h1"], 64)
        cast_store_scaled(ffn_e["h2"], wn["h2"], 96)
        cast_store_scaled(ffn_e["w3"], wn["w3"], 128, extra=0.5)

        def transpose_weight(nm):
            O, I = wn[nm].shape
            for j in range(I // 128):
                for o0 in range(0, O, 1024):
                    w = min(1024, O - o0)
                    sync.dma_start(out=stg[:, 0:w],
                                   in_=wn[nm].ap()[o0:o0 + w, j * 128:(j + 1) * 128],
                                   transpose=True)
                    if nm == "w3":
                        for h0 in range(0, w, 1024):
                            sync.dma_start(out=wt[nm].ap()[j // 2, :, j % 2,
                                                           0:1024] if w == 1024 else
                                           wt[nm].ap()[j // 2, :, j % 2, :],
                                           in_=stg[:, h0:h0 + 1024])
                    else:
                        for gg in range(w // 512):
                            g = (o0 // 512) + gg
                            sync.dma_start(out=wt[nm].ap()[j // 4, g, :, j % 4, :],
                                           in_=stg[:, gg * 512:(gg + 1) * 512])

        for nm in ("w1", "w2", "h1", "h2", "w3"):
            transpose_weight(nm)

        for dst, src in ((tri16, tri16_e), (trirep, trirep_e),
                         (sixcol, sixcol_e), (onesrow, onesrow_e), (ident, ident_e),
                         (bgk_bc, bgk_e), (gam_bc, gam_e), (bet_bc, bet_e)):
            sync.dma_start(out=dst[:, :], in_=src.ap())
        dv.memset(S[:, :], 0.0)

        def ln_to(dst_ap, src_ap, ev_guard=False):
            def f(e):
                e.bn_stats(stat[:, 0:6], src_ap[:, 0:512])
                return e.bn_stats(stat[:, 6:12], src_ap[:, 512:1024])
            if ev_guard:
                dv.waitsem(ev_sem, rec.ev)
            dv.group(f)
            dv.bn_aggr(mv[:, :], stat[:, :])
            dv.tensor_scalar_add(rstd[:, :], mv[:, 1:2], EPS)
            ac.activation(rstd[:, :], rstd[:, :], AF.Ln)
            ac.activation(rstd[:, :], rstd[:, :], AF.Exp, scale=-0.5)
            dv.scalar_tensor_tensor(nmean[:, :], mv[:, 0:1], -1.0, rstd[:, :], ALU.mult, ALU.mult)
            dv.tensor_scalar(dst_ap, src_ap, rstd[:, 0:1], nmean[:, 0:1], ALU.mult, ALU.add)

        def transpose_chunk(fm_sb, tok_f32, cc):
            def f(e):
                last = None
                for j in range(8):
                    last = e.transpose(psS[:, j * 128:(j + 1) * 128],
                                       tok_f32[:, j * 128:(j + 1) * 128], ident[:, :])
                return last
            pe.group(f)
            dv.tensor_copy(
                fm_sb[:, :].rearrange("p (j t) -> p j t", j=8)[:, :, cc * 128:(cc + 1) * 128],
                psS[:, :].rearrange("p (j c) -> p j c", j=8))

        def aproj(fm_sb, cc, wext, ps):
            for j2 in range(4):
                j0 = 2 * j2
                slot, pre, post = astr.load(
                    asl[:, (astr.n % 2) * 2 * D:(astr.n % 2 + 1) * 2 * D],
                    wext.ap()[j2, :, :, :].rearrange("p kk c -> p (kk c)"))

                def f(e, _sl=slot, _j0=j0):
                    last = None
                    for kk in range(2):
                        j = _j0 + kk
                        base = _sl * 2 * D + kk * D
                        lhsT = fm_sb[:, j * T + cc * 128: j * T + (cc + 1) * 128]
                        e.matmul(ps[:, 0:512], lhsT, asl[:, base:base + 512],
                                 start=(j == 0), stop=(j == 7))
                        last = e.matmul(ps[:, 512:1024], lhsT, asl[:, base + 512:base + 1024],
                                        start=(j == 0), stop=(j == 7))
                    return last
                pe.group(f, pre=(pre,), post=(post,))

        def wout_mm(cc, ps):
            for j2 in range(4):
                j0 = 2 * j2
                slot, pre, post = astr.load(
                    asl[:, (astr.n % 2) * 2 * D:(astr.n % 2 + 1) * 2 * D],
                    wa["Wout"].ap()[j2, :, :, :].rearrange("p kk c -> p (kk c)"))

                def f(e, _sl=slot, _j0=j0):
                    last = None
                    for kk in range(2):
                        j = _j0 + kk
                        base = _sl * 2 * D + kk * D
                        lhsT = yfm[:, j * 128:(j + 1) * 128]
                        e.matmul(ps[:, 0:512], lhsT, asl[:, base:base + 512],
                                 start=(j == 0), stop=(j == 7))
                        last = e.matmul(ps[:, 512:1024], lhsT, asl[:, base + 512:base + 1024],
                                        start=(j == 0), stop=(j == 7))
                    return last
                pe.group(f, pre=(pre,), post=(post,))

        def mm2S(lhsT, rhs0, rhs1, rows=(0, 128)):
            def f(e):
                e.matmul(psS[rows[0]:rows[1], 0:512], lhsT, rhs0, start=True, stop=True)
                return e.matmul(psS[rows[0]:rows[1], 512:1024], lhsT, rhs1, start=True, stop=True)
            pe.group(f)

        FLUSH_PER_GROUP = 6

        def stream_layer(wtname, nI, rhs_sb, evac, rhs_tk=None):
            wte = wt[wtname]
            for g in range(wte.shape[1]):
                for k4 in range(nI // 4):
                    k0 = 4 * k4
                    slot, pre, post = wstr.load(
                        wsl[:, (wstr.n % 5) * 2048:(wstr.n % 5 + 1) * 2048],
                        wte.ap()[k4, g, :, :, :].rearrange("p kk c -> p (kk c)"))
                    evneed = rec.ev
                    pres = [pre]
                    if rhs_tk is not None and g == 0 and k4 == 0:
                        pres.append((tk_sem, rhs_tk))

                    def f(e, _sl=slot, _k0=k0, _nI=nI, _rhs=rhs_sb, _ev=evneed):
                        if _ev and _k0 == 0:
                            e.wait_ge(ev_sem, _ev)
                        last = None
                        for kk in range(4):
                            k = _k0 + kk
                            for u in range(4):
                                last = e.matmul(
                                    ps4b[:, u * 512:(u + 1) * 512],
                                    wsl[:, _sl * 2048 + kk * 512 + u * 128:
                                        _sl * 2048 + kk * 512 + (u + 1) * 128],
                                    _rhs[:, k * T:(k + 1) * T],
                                    start=(k == 0), stop=(k == _nI - 1))
                        return last
                    pe.group(f, pre=tuple(pres), post=(post,), tick=False, tkwait=False)
                    rec.flush(2)
                evac(g)
                rec.flush(FLUSH_PER_GROUP)

        def ffn_pass(src_fm, out_cb):
            rhs_ready = rec.val
            def ev_w1(g):
                evp(ac, lambda e, _g=g: e.activation(
                    h_a[:, 4 * _g * T:(4 * _g + 4) * T], ps4b[:, :], AF.Silu),
                    pre=((wstr.free_sem, wstr.n),))
            stream_layer("w1", 8, src_fm, ev_w1, rhs_tk=rhs_ready)

            def ev_w2(g):
                evp(dv, lambda e, _g=g: e.tensor_mul(
                    h_b[:, 4 * _g * T:(4 * _g + 4) * T], ps4b[:, :],
                    h_a[:, 4 * _g * T:(4 * _g + 4) * T]),
                    pre=((wstr.free_sem, wstr.n), (ev_sem, rec.ev),
                         (w3str.free_sem, w3str.n)))
            stream_layer("w2", 8, src_fm, ev_w2, rhs_tk=rhs_ready)

            def ev_h1(g):
                evp(ac, lambda e, _g=g: e.activation(
                    h_a[:, 4 * _g * T:(4 * _g + 4) * T], ps4b[:, :], AF.Gelu),
                    pre=((wstr.free_sem, wstr.n),))
            stream_layer("h1", NKH, h_b, ev_h1)

            def ev_h2(g):
                evp(ac, lambda e, _g=g: e.activation(
                    h_b[:, 4 * _g * T:(4 * _g + 4) * T], ps4b[:, :], AF.Gelu),
                    pre=((wstr.free_sem, wstr.n), (w3str.free_sem, w3str.n)))
            stream_layer("h2", NKH, h_a, ev_h2)

            for cc in range(NCH):
                for m2 in range(NKH // 2):
                    m0 = 2 * m2
                    slot, pre, post = w3str.load(
                        w3sl[:, (w3str.n % 2) * 2 * D:(w3str.n % 2 + 1) * 2 * D],
                        wt["w3"].ap()[m2, :, :, :].rearrange("p kk c -> p (kk c)"))
                    evneed = rec.ev

                    def f(e, _sl=slot, _m0=m0, _cc=cc, _ev=evneed):
                        if _ev and _m0 == 0:
                            e.wait_ge(ev_sem, _ev)
                        last = None
                        for kk in range(2):
                            m = _m0 + kk
                            base = _sl * 2 * D + kk * D
                            lhsT = h_b[:, m * T + _cc * 128: m * T + (_cc + 1) * 128]
                            e.matmul(psD[:, 0:512], lhsT, w3sl[:, base:base + 512],
                                     start=(m == 0), stop=(m == NKH - 1))
                            last = e.matmul(psD[:, 512:1024], lhsT, w3sl[:, base + 512:base + 1024],
                                            start=(m == 0), stop=(m == NKH - 1))
                        return last
                    pe.group(f, pre=(pre,), post=(post,), tick=False, tkwait=False)
                out_cb(cc)
                rec.flush(2 * FLUSH_PER_GROUP)

        def scan_chunk(cc, own, strip):
            tok0 = strip * T + cc * 128
            gp.dma_start(out=r1d[:, :], in_=r1sc.ap()[tok0:tok0 + 128, :])
            ln_to(xnt[:, :], r1d[:, :])
            transpose_chunk(xn2_fm, xnt, cc)

            def f_z1(e):
                last = None
                for j in range(8):
                    last = e.matmul(psS[0:16, 0:128], agk1_sb[:, j * 16:(j + 1) * 16],
                                    xn2_fm[:, j * T + cc * 128: j * T + (cc + 1) * 128],
                                    start=(j == 0), stop=(j == 7))
                return last
            pe.group(f_z1)
            dv.tensor_copy(z1_bf[:, :], psS[0:16, 0:128])
            mm2S(z1_bf[:, :], agk2_sb[:, 0:512], agk2_sb[:, 512:1024])
            dv.tensor_add(zb[:, :], psS[:, :], bgk_bc[:, :])
            ac.activation(t1[:, :], zb[:, :], AF.Abs)
            ac.activation(t2[:, :], t1[:, :], AF.Exp, scale=-1.0)
            dv.tensor_scalar_add(t1[:, :], t2[:, :], 1.0)
            ac.activation(t2[:, :], t1[:, :], AF.Ln)
            dv.tensor_scalar_min(t1[:, :], zb[:, :], 0.0)
            dv.tensor_sub(g_bf[:, :], t1[:, :], t2[:, :])

            mm2S(tri16[:, :], g_bf[:, 0:512], g_bf[:, 512:1024])
            dv.tensor_copy(cumG[:, :], psS[:, :])
            dv.tensor_copy(mid_bf[:, :], cumG[64:65, :])

            def f_sumg(e):
                e.matmul(psS[0:1, 0:512], sixcol[:, :], g_bf[:, 0:512], start=True, stop=True)
                return e.matmul(psS[0:1, 512:1024], sixcol[:, :], g_bf[:, 512:1024],
                                start=True, stop=True)
            pe.group(f_sumg)
            dv.tensor_sub(xnt[0:1, :], psS[0:1, :], mid_bf[:, :])
            mm2S(onesrow[0:1, :], mid_bf[0:1, 0:512], mid_bf[0:1, 512:1024])
            dv.tensor_sub(cumGc[:, :], cumG[:, :], psS[:, :])
            ac.activation(lam[:, :], cumGc[:, :], AF.Exp)
            ac.activation(dinv[:, :], cumGc[:, :], AF.Exp, scale=-1.0)
            ac.activation(emu_bf[:, :], mid_bf[:, :], AF.Exp)
            ac.activation(f_bf[:, :], xnt[0:1, :], AF.Exp)

            mm2S(onesrow[0:1, 0:64], emu_bf[0:1, 0:512], emu_bf[0:1, 512:1024], rows=(0, 64))
            dv.tensor_mul(S0x[0:64, :], S[:, :], psS[0:64, :])
            dv.tensor_copy(S0x_bf[:, :], S0x[0:64, :])

            aproj(xn2_fm, cc, wa["Wv"], psS)
            dv.tensor_mul(vt_bf[:, :], psS[:, :], dinv[:, :])
            aproj(xn2_fm, cc, wa["Wk"], psS)
            dv.tensor_copy(ktok[:, :], psS[:, :])
            if own:
                dv.tensor_copy(ktf[:, :], psS[:, :])

            def f_ds(e):
                last = None
                for h in range(NH):
                    last = e.matmul(psS[0:64, h * 64:(h + 1) * 64],
                                    ktok[:, h * 64:(h + 1) * 64], vt_bf[:, h * 64:(h + 1) * 64],
                                    start=(h % 8 == 0), stop=(h % 8 == 7))
                return last
            pe.group(f_ds)
            dv.tensor_add(S[:, :], S0x[0:64, :], psS[0:64, :])
            mm2S(onesrow[0:1, 0:64], f_bf[0:1, 0:512], f_bf[0:1, 512:1024], rows=(0, 64))
            dv.tensor_mul(S[:, :], S[:, :], psS[0:64, :])

            if not own:
                return

            def tr8(e, src_sb):
                last = None
                for j in range(8):
                    last = e.transpose(psS[:, j * 128:(j + 1) * 128],
                                       src_sb[:, j * 128:(j + 1) * 128], ident[:, :])
                return last
            pe.group(lambda e: tr8(e, ktf))
            dv.tensor_copy(kfm[0:64, 0:1024].rearrange("p (j c) -> p j c", j=8),
                           psS[0:64, :].rearrange("p (j c) -> p j c", j=8))
            dv.tensor_copy(kfm[0:64, 1024:2048].rearrange("p (j c) -> p j c", j=8),
                           psS[64:128, :].rearrange("p (j c) -> p j c", j=8))
            aproj(xn2_fm, cc, wa["Wq"], psS)
            dv.tensor_copy(qtf[:, :], psS[:, :])
            pe.group(lambda e: tr8(e, qtf))
            dv.tensor_copy(qfm[0:64, 0:1024].rearrange("p (j c) -> p j c", j=8),
                           psS[0:64, :].rearrange("p (j c) -> p j c", j=8))
            dv.tensor_copy(qfm[0:64, 1024:2048].rearrange("p (j c) -> p j c", j=8),
                           psS[64:128, :].rearrange("p (j c) -> p j c", j=8))

            for q4 in range(4):
                def f_a(e, _q=q4):
                    last = None
                    for hh in range(4):
                        h = _q * 4 + hh
                        b = (h >> 1) + 8 * (h & 1)
                        last = e.matmul(psS[:, hh * 128:(hh + 1) * 128],
                                        kfm[0:64, b * 128:(b + 1) * 128],
                                        qfm[0:64, b * 128:(b + 1) * 128],
                                        start=(hh == 0), stop=(hh == 3))
                    return last
                pe.group(f_a)
                dv.tensor_mul(am[:, :], psS[:, 0:512], trirep[:, :])

                def f_oi(e, _q=q4):
                    last = None
                    for hh in range(4):
                        h = _q * 4 + hh
                        b = (h >> 1) + 8 * (h & 1)
                        col = 512 + 64 * hh
                        e.matmul(psS[:, col:col + 64], am[:, hh * 128:(hh + 1) * 128],
                                 vt_bf[:, h * 64:(h + 1) * 64],
                                 start=(hh == 0), stop=False)
                        last = e.matmul(psS[:, col:col + 64],
                                        qfm[0:64, b * 128:(b + 1) * 128],
                                        S0x_bf[0:64, h * 64:(h + 1) * 64],
                                        start=False, stop=(hh == 3))
                    return last
                pe.group(f_oi)
                dv.tensor_mul(out_pre[:, q4 * 256:(q4 + 1) * 256], psS[:, 512:768],
                              lam[:, q4 * 256:(q4 + 1) * 256])

            ln_to(t1[:, :], out_pre[:, :])
            dv.tensor_mul(t2[:, :], t1[:, :], gam_bc[:, :])
            dv.tensor_add(t2[:, :], t2[:, :], bet_bc[:, :])
            aproj(xn2_fm, cc, wa["Wg"], psS)
            ac.activation(t1[:, :], psS[:, :], AF.Silu)
            dv.tensor_mul(ytf[:, :], t2[:, :], t1[:, :])
            pe.group(lambda e: tr8(e, ytf))
            dv.tensor_copy(yfm[:, :].rearrange("p (j c) -> p j c", j=8),
                           psS[:, :].rearrange("p (j c) -> p j c", j=8))
            wout_mm(cc, psS)
            own_tok0 = (strip - NPRE) * T + cc * 128
            gp.dma_start(out=r1d[:, :], in_=r1sc.ap()[tok0:tok0 + 128, :])
            dv.tensor_add(xnt[:, :], r1d[:, :], psS[:, :])
            gp.dma_start(out=r2sc.ap()[own_tok0:own_tok0 + 128, :], in_=xnt[:, :])

        def load_x(s, cc):
            tok0 = s * T + cc * 128
            slot = cc % 2
            gp.group(lambda e, _t=tok0, _sl=slot: e.dma_start(
                out=x_sb[:, _sl * D:(_sl + 1) * D], in_=xin.ap()[_t:_t + 128, :]),
                pre=((ev_sem, rec.ev),), post=((xld_sem, 16),))
            xld_n[0] += 1
            return xld_n[0]

        def record_ffn1(s):
            for cc in range(NCH):
                n = load_x(s, cc)
                dv.waitsem(xld_sem, 16 * n)
                ln_to(xnt[:, :], x_sb[:, (cc % 2) * D:(cc % 2 + 1) * D],
                      ev_guard=(cc == 0))
                transpose_chunk(xn_fm, xnt, cc)

            def out1(cc, _s=s):
                n = load_x(_s, cc)
                evp(dv, lambda e, _cc=cc: e.tensor_add(
                    r1c[:, :], psD[:, :], x_sb[:, (_cc % 2) * D:(_cc % 2 + 1) * D]),
                    pre=((w3str.free_sem, w3str.n), (xld_sem, 16 * n),
                         (r1st_sem, 16 * (_s * NCH + cc))))
                tok0 = _s * T + cc * 128
                gp.group(lambda e, _t=tok0: e.dma_start(
                    out=r1sc.ap()[_t:_t + 128, :], in_=r1c[:, :]),
                    pre=((ev_sem, rec.ev),), post=((r1st_sem, 16),))

            ffn_pass(xn_fm, out1)

        def record_ffn2(s):
            for cc in range(NCH):
                own_tok0 = (s - NPRE) * T + cc * 128
                gp.group(lambda e, _t=own_tok0: e.dma_start(
                    out=r1c[:, :], in_=r2sc.ap()[_t:_t + 128, :]),
                    pre=((ev_sem, rec.ev),))
                ln_to(xnt[:, :], r1c[:, :], ev_guard=(cc == 0))
                transpose_chunk(xn_fm, xnt, cc)

            def out2(cc, _s=s):
                dv.waitsem(w3str.free_sem, w3str.n)
                evp(dv, lambda e: e.tensor_copy(w3tmp[:, :], psD[:, :]), tkwait=True)
                own_tok0 = (_s - NPRE) * T + cc * 128
                gp.group(lambda e, _t=own_tok0: e.dma_start(
                    out=r1c[:, :], in_=r2sc.ap()[_t:_t + 128, :]),
                    pre=((ev_sem, rec.ev),))
                evp(dv, lambda e: e.tensor_add(xnt[:, :], r1c[:, :], w3tmp[:, :]),
                    pre=((ev_sem, rec.ev),), tkwait=True)
                gp.group(lambda e, _t=own_tok0: e.dma_start(
                    out=out_ext.ap()[_t:_t + 128, :], in_=xnt[:, :]),
                    pre=((ev_sem, rec.ev),))

            ffn_pass(xn_fm, out2)

        prev_own = False
        for s in range(NSTRIP):
            own = s >= NPRE
            record_ffn1(s)
            rec.flush_all()
            if prev_own:
                record_ffn2(s - 1)
            rec.defer = True
            for cc in range(NCH):
                scan_chunk(cc, own, s)
            rec.defer = False
            prev_own = own
        rec.flush_all()
        record_ffn2(NSTRIP - 1)
        rec.flush_all()

        final_ticket = rec.val
        final_ev = rec.ev

        def replay(engname):
            skip_self = engname == "pe"

            def f(eng):
                low = 0
                qcnt = 0
                for (w, fn, inc, pre, post) in rec.items[engname]:
                    if w is not None and w > low:
                        eng.wait_ge(tk_sem, w)
                        low = w
                    for (sem, val) in pre:
                        eng.wait_ge(sem, val)
                    inst = fn(eng)
                    if inc == 16:
                        qs = qsems[engname]
                        assert inst is not None
                        inst.then_inc(qs, 16)
                        qcnt += 16
                        eng.wait_ge(qs, qcnt)
                        eng.nop().then_inc(tk_sem, 16)
                        for (sem, val) in post:
                            eng.nop().then_inc(sem, val)
                    elif inc or post:
                        assert inst is not None, f"group must return an instruction ({engname})"
                        updates = ([(tk_sem, inc)] if inc else []) + list(post)
                        inst.then_inc(*updates[0])
                        for (sem, val) in updates[1:]:
                            eng.nop().then_inc(sem, val)
                    if skip_self:
                        low += inc
                if engname == "gp":
                    eng.wait_ge(tk_sem, final_ticket)
                    eng.wait_ge(ev_sem, final_ev)
            return f

        block.sync(replay("sync"))
        block.gpsimd(replay("gp"))
        block.tensor(replay("pe"))
        block.vector(replay("dv"))
        block.scalar(replay("ac"))
        for k in list(rec.items):
            rec.items[k] = []
    return nc


def _consts(cfg):
    import ml_dtypes
    tri = np.tril(np.ones((128, 128), dtype=np.float32)).T
    bf = lambda a: a.astype(ml_dtypes.bfloat16)
    return dict(
        tri16_c=bf(tri / 16.0),
        trirep_c=bf(np.tile(tri, (1, 4))),
        sixcol_c=bf(np.full((128, 1), 1 / 16.0, dtype=np.float32)),
        onesrow_c=bf(np.ones((1, 128), dtype=np.float32)),
        ident_c=np.eye(128, dtype=np.float32),
    )


def make_in_maps(cfg, x_full, W):
    T, NSTRIP, NOWN = cfg["T"], cfg["n_strips"], cfg["own_strips"]
    SEQ = NSTRIP * T
    SLAB = NOWN * T
    consts = _consts(cfg)
    f32 = np.float32

    def fm_scales(s):
        return np.ascontiguousarray(s.reshape(-1, 128).T.astype(f32))

    scales = np.concatenate([fm_scales(W["ffn_w1_s"]), fm_scales(W["ffn_w2_s"]),
                             fm_scales(W["ffn_h1_s"]), fm_scales(W["ffn_h2_s"]),
                             fm_scales(W["ffn_w3_s"])], axis=1)
    bcast = lambda v: np.ascontiguousarray(np.broadcast_to(v.astype(f32), (128, D)))
    shared = dict(
        Wq=W["attn_Wq"], Wk=W["attn_Wk"], Wv=W["attn_Wv"], Wg=W["attn_Wg"],
        Wout=W["attn_Wout"], Wgk1=W["attn_Wgk1"], Wgk2=W["attn_Wgk2"],
        w1=W["ffn_w1_w"], w2=W["ffn_w2_w"], h1=W["ffn_h1_w"], h2=W["ffn_h2_w"],
        w3=W["ffn_w3_w"],
        scales=scales,
        bgk_bc=bcast(W["attn_bgk2"]),
        gam_bc=bcast(W["attn_gamma"]),
        bet_bc=bcast(W["attn_beta"]),
        **consts,
    )
    in_maps = []
    for core in range(8):
        b, p = divmod(core, 4)
        take = (p + 1) * SLAB
        xi = np.zeros((SEQ, D), dtype=f32)
        xi[SEQ - take:] = x_full[b, :take]
        m = dict(shared)
        m["xin"] = xi
        in_maps.append(m)
    return in_maps


def kernel(**inputs):
    cfg = CFG_FULL
    W = {
        "attn_Wq": np.asarray(inputs["L0_attn_Wq"]),
        "attn_Wk": np.asarray(inputs["L0_attn_Wk"]),
        "attn_Wv": np.asarray(inputs["L0_attn_Wv"]),
        "attn_Wg": np.asarray(inputs["L0_attn_Wg"]),
        "attn_Wout": np.asarray(inputs["L0_attn_Wout"]),
        "attn_Wgk1": np.asarray(inputs["L0_attn_Wgk1"]),
        "attn_Wgk2": np.asarray(inputs["L0_attn_Wgk2"]),
        "attn_bgk2": np.asarray(inputs["L0_attn_bgk2"]),
        "attn_gamma": np.asarray(inputs["L0_attn_gamma"]),
        "attn_beta": np.asarray(inputs["L0_attn_beta"]),
        "ffn_w1_w": np.asarray(inputs["L0_ffn_w1_w"]),
        "ffn_w1_s": np.asarray(inputs["L0_ffn_w1_s"]),
        "ffn_w2_w": np.asarray(inputs["L0_ffn_w2_w"]),
        "ffn_w2_s": np.asarray(inputs["L0_ffn_w2_s"]),
        "ffn_h1_w": np.asarray(inputs["L0_ffn_h1_w"]),
        "ffn_h1_s": np.asarray(inputs["L0_ffn_h1_s"]),
        "ffn_h2_w": np.asarray(inputs["L0_ffn_h2_w"]),
        "ffn_h2_s": np.asarray(inputs["L0_ffn_h2_s"]),
        "ffn_w3_w": np.asarray(inputs["L0_ffn_w3_w"]),
        "ffn_w3_s": np.asarray(inputs["L0_ffn_w3_s"]),
    }
    x_full = np.asarray(inputs["L0_attn_x"], dtype=np.float32)
    in_maps = make_in_maps(cfg, x_full, W)
    del x_full, W
    import gc
    gc.collect()
    nc = build_program(cfg)
    res = run_bass_kernel_spmd(nc, in_maps, core_ids=list(range(8)))
    SLAB = cfg["own_strips"] * cfg["T"]
    out = np.empty((2, 16384, D), dtype=np.float32)
    for core in range(8):
        b, p = divmod(core, 4)
        out[b, p * SLAB:(p + 1) * SLAB] = np.asarray(res.results[core]["out"])
    return out


# version 12
# speedup vs baseline: 1.0710x; 1.0581x over previous
import numpy as np
import concourse.bass as bass
import concourse.mybir as mybir
from concourse.bass_utils import run_bass_kernel_spmd
from contextlib import ExitStack

F32 = mybir.dt.float32
BF16 = mybir.dt.bfloat16
AF = mybir.ActivationFunctionType
ALU = mybir.AluOpType

D = 1024
DFF = 4096
NH, HD = 16, 64
EPS = 1e-5

CFG_FULL = dict(T=512, n_strips=32, own_strips=8)

ENGS = ("sync", "gp", "pe", "dv", "ac")


class Rec:
    def __init__(self):
        self.items = {k: [] for k in ENGS}
        self.val = 0
        self.ev = 0
        self.pending = []
        self.defer = False

    def emit(self, eng, fn, inc, pre=(), post=(), evflag=False, tkwait=True):
        if self.defer:
            self.pending.append((eng, fn, inc, tuple(pre), tuple(post), evflag, tkwait))
            return
        self._commit(eng, fn, inc, pre, post, evflag, tkwait)

    def _commit(self, eng, fn, inc, pre, post, evflag, tkwait=True):
        if evflag:
            self.ev += 1
        self.items[eng].append((self.val if tkwait else None, fn, inc,
                                tuple(pre), tuple(post)))
        self.val += inc

    def flush(self, k):
        n = 0
        while self.pending and n < k:
            item = self.pending.pop(0)
            self._commit(*item)
            n += 1

    def flush_all(self):
        self.flush(len(self.pending))


class EngRec:
    def __init__(self, rec, name, inc):
        object.__setattr__(self, "_rec", rec)
        object.__setattr__(self, "_name", name)
        object.__setattr__(self, "_inc", inc)

    def __getattr__(self, meth):
        rec, name, inc = self._rec, self._name, self._inc

        def call(*args, **kwargs):
            rec.emit(name, lambda e: getattr(e, meth)(*args, **kwargs), inc)

        return call

    def group(self, fn, pre=(), post=(), tick=True, evflag=False, tkwait=True):
        self._rec.emit(self._name, fn, self._inc if tick else 0, pre, post,
                       evflag, tkwait)

    def waitsem(self, sem, val):
        self._rec.emit(self._name, lambda e: e.wait_ge(sem, val), 0)


class Stream:

    def __init__(self, rec, nc, stack, name, qname, nbuf=2):
        self.rec = rec
        self.qname = qname
        self.nbuf = nbuf
        self.load_sems = [stack.enter_context(nc.semaphore(f"{name}_ld{i}"))
                          for i in range(nbuf)]
        self.free_sem = stack.enter_context(nc.semaphore(f"{name}_fr"))
        self.n = 0

    def load(self, out_ap, in_ap):
        slot = self.n % self.nbuf
        need_free = self.n - self.nbuf + 1
        ls, fs = self.load_sems[slot], self.free_sem

        def fn(e, _nf=need_free, _o=out_ap, _i=in_ap):
            if _nf > 0:
                e.wait_ge(fs, _nf)
            return e.dma_start(out=_o, in_=_i)

        self.rec.emit(self.qname, fn, 0, (), ((ls, 16),))
        self.n += 1
        nuse = (self.n - 1) // self.nbuf + 1
        return slot, (ls, 16 * nuse), (fs, 1)


def build_program(cfg):
    T = cfg["T"]
    NSTRIP = cfg["n_strips"]
    NOWN = cfg["own_strips"]
    NPRE = NSTRIP - NOWN
    SEQ = NSTRIP * T
    NCH = T // 128
    OWN_TOK = NOWN * T
    NKH = DFF // 128

    nc = bass.Bass("TRN2", target_bir_lowering=False)

    xin = nc.dram_tensor("xin", [SEQ, D], F32, kind="ExternalInput")
    wq_e = nc.dram_tensor("Wq", [D, D], F32, kind="ExternalInput")
    wk_e = nc.dram_tensor("Wk", [D, D], F32, kind="ExternalInput")
    wv_e = nc.dram_tensor("Wv", [D, D], F32, kind="ExternalInput")
    wg_e = nc.dram_tensor("Wg", [D, D], F32, kind="ExternalInput")
    wo_e = nc.dram_tensor("Wout", [D, D], F32, kind="ExternalInput")
    wgk1_e = nc.dram_tensor("Wgk1", [D, 16], F32, kind="ExternalInput")
    wgk2_e = nc.dram_tensor("Wgk2", [16, D], F32, kind="ExternalInput")
    ffn_e = {nm: nc.dram_tensor(nm, shape, F32, kind="ExternalInput")
             for nm, shape in (("w1", [DFF, D]), ("w2", [DFF, D]), ("h1", [DFF, DFF]),
                               ("h2", [DFF, DFF]), ("w3", [D, DFF]))}
    scales_e = nc.dram_tensor("scales", [128, 136], F32, kind="ExternalInput")
    bgk_e = nc.dram_tensor("bgk_bc", [128, D], F32, kind="ExternalInput")
    gam_e = nc.dram_tensor("gam_bc", [128, D], F32, kind="ExternalInput")
    bet_e = nc.dram_tensor("bet_bc", [128, D], F32, kind="ExternalInput")
    tri16_e = nc.dram_tensor("tri16_c", [128, 128], BF16, kind="ExternalInput")
    trirep_e = nc.dram_tensor("trirep_c", [128, 512], BF16, kind="ExternalInput")
    sixcol_e = nc.dram_tensor("sixcol_c", [128, 1], BF16, kind="ExternalInput")
    onesrow_e = nc.dram_tensor("onesrow_c", [1, 128], BF16, kind="ExternalInput")
    ident_e = nc.dram_tensor("ident_c", [128, 128], F32, kind="ExternalInput")
    out_ext = nc.dram_tensor("out", [OWN_TOK, D], F32, kind="ExternalOutput")

    wn = {nm: nc.dram_tensor(f"wn_{nm}", list(ffn_e[nm].shape), BF16) for nm in ffn_e}
    wt = {"w1": nc.dram_tensor("wt_w1", [4, 8, 128, 2, 512], BF16),
          "w2": nc.dram_tensor("wt_w2", [4, 8, 128, 2, 512], BF16),
          "h1": nc.dram_tensor("wt_h1", [16, 8, 128, 2, 512], BF16),
          "h2": nc.dram_tensor("wt_h2", [16, 8, 128, 2, 512], BF16),
          "w3": nc.dram_tensor("wt_w3", [16, 128, 2, 1024], BF16)}
    wa = {nm: nc.dram_tensor(f"wa_{nm}", [4, 128, 2, 1024], BF16)
          for nm in ("Wq", "Wk", "Wv", "Wg", "Wout")}
    r1sc = nc.dram_tensor("r1sc", [SEQ, D], F32)
    r2sc = nc.dram_tensor("r2sc", [OWN_TOK, D], F32)

    with ExitStack() as stack:
        E = stack.enter_context

        def sb(name, shape, dt):
            return E(nc.sbuf_tensor(name, shape, dt))

        h_a = sb("h_a", [128, NKH * T], BF16)
        h_b = sb("h_b", [128, NKH * T], BF16)
        x_sb = sb("x_sb", [128, NCH * D], F32)
        xn_fm = sb("xn_fm", [128, 8 * T], BF16)
        xn2_fm = sb("xn2_fm", [128, 8 * T], BF16)
        xnt = sb("xnt", [128, D], F32)
        r1c = sb("r1c", [128, D], F32)
        wsl = sb("wsl", [128, 5 * 1024], BF16)
        w3sl = sb("w3sl", [128, 3 * 2 * D], BF16)
        asl = sb("asl", [128, 3 * 2 * D], BF16)
        agk1_sb = sb("agk1_sb", [128, 8 * 16], BF16)
        agk2_sb = sb("agk2_sb", [16, D], BF16)
        scales = sb("scales_sb", [128, 136], F32)
        tri16 = sb("tri16", [128, 128], BF16)
        trirep = sb("trirep", [128, 512], BF16)
        sixcol = sb("sixcol", [128, 1], BF16)
        onesrow = sb("onesrow", [1, 128], BF16)
        ident = sb("ident", [128, 128], F32)
        bgk_bc = sb("bgk_bc_s", [128, D], F32)
        gam_bc = sb("gam_bc_s", [128, D], F32)
        bet_bc = sb("bet_bc_s", [128, D], F32)
        S = sb("S", [64, D], F32)
        S0x_bf = sb("S0x_bf", [64, D], BF16)
        zb = sb("zb", [128, D], F32)
        t1 = sb("t1", [128, D], F32)
        t2 = sb("t2", [128, D], F32)
        cumG = sb("cumG", [128, D], F32)
        g_bf = sb("g_bf", [128, D], BF16)
        vt_bf = sb("vt_bf", [128, D], BF16)
        ktok = sb("ktok", [128, D], BF16)
        ktf = sb("ktf", [128, D], F32)
        kfm = sb("kfm", [64, 2 * D], BF16)
        qfm = sb("qfm", [64, 2 * D], BF16)
        am = sb("am", [128, 512], BF16)
        z1_bf = sb("z1_bf", [16, 128], BF16)
        mid_bf = sb("mid_bf", [1, D], BF16)
        emu_bf = sb("emu_bf", [1, D], BF16)
        stat = sb("stat", [128, 12], F32)
        mv = sb("mv", [128, 2], F32)
        rstd = sb("rstd", [128, 1], F32)
        nmean = sb("nmean", [128, 1], F32)
        stg = sb("stg", [128, 2048], BF16)
        stgf = t1

        out_pre = zb
        r1d = zb
        lam = t1
        dinv = t2
        cumGc = cumG
        S0x = cumG
        f_bf = mid_bf
        yfm = g_bf
        qtf = ktf
        ytf = ktf
        w3tmp = ktf

        ps4b = E(nc.psum_tensor("ps4b", [128, 2048], F32))
        psD = E(nc.psum_tensor("psD", [128, 1024], F32))
        psS = E(nc.psum_tensor("psS", [128, 1024], F32))

        tk_sem = E(nc.semaphore("tk"))
        ev_sem = E(nc.semaphore("ev"))
        qsems = {q: E(nc.semaphore(f"q_{q}")) for q in ("sync", "gp", "ac")}
        block = E(nc.Block())

        rec = Rec()
        sync = EngRec(rec, "sync", 16)
        gp = EngRec(rec, "gp", 16)
        pe = EngRec(rec, "pe", 1)
        dv = EngRec(rec, "dv", 1)
        ac = EngRec(rec, "ac", 1)

        wstr = Stream(rec, nc, stack, "wst", "sync", nbuf=5)
        w3str = Stream(rec, nc, stack, "w3st", "sync", nbuf=3)
        astr = Stream(rec, nc, stack, "ast", "gp", nbuf=3)

        def evp(engrec, fn, pre=()):
            engrec.group(fn, pre=pre, post=((ev_sem, 1),), tick=False, evflag=True)

        sync.dma_start(out=scales[:, :], in_=scales_e.ap())

        def cast_store(ext, scratch):
            for j in range(8):
                view = stg[:, 0:D]
                gp.dma_start(out=view, in_=ext.ap()[j * 128:(j + 1) * 128, :])
                sync.dma_start(out=scratch.ap()[j // 2, :, j % 2, :], in_=view)

        def cast_store_scaled(ext, scratch, scol0, extra=1.0):
            rows, cols = ext.shape
            nch = rows // 128
            for j in range(nch):
                for h0 in range(0, cols, 1024):
                    w = min(1024, cols - h0)
                    gp.dma_start(out=stgf[:, 0:w],
                                 in_=ext.ap()[j * 128:(j + 1) * 128, h0:h0 + w])
                    dv.tensor_scalar(stg[:, 0:w], stgf[:, 0:w],
                                     scales[:, scol0 + j:scol0 + j + 1], extra,
                                     ALU.mult, ALU.mult)
                    sync.dma_start(out=scratch.ap()[j * 128:(j + 1) * 128, h0:h0 + w],
                                   in_=stg[:, 0:w])

        for ext, nm in ((wq_e, "Wq"), (wk_e, "Wk"), (wv_e, "Wv"), (wg_e, "Wg"), (wo_e, "Wout")):
            cast_store(ext, wa[nm])
        gp.dma_start(out=agk1_sb[:, :].rearrange("p (j o) -> p j o", j=8),
                     in_=wgk1_e.ap().rearrange("(j p) o -> p j o", p=128))
        gp.dma_start(out=agk2_sb[:, :], in_=wgk2_e.ap())
        cast_store_scaled(ffn_e["w1"], wn["w1"], 0)
        cast_store_scaled(ffn_e["w2"], wn["w2"], 32)
        cast_store_scaled(ffn_e["h1"], wn["h1"], 64)
        cast_store_scaled(ffn_e["h2"], wn["h2"], 96)
        cast_store_scaled(ffn_e["w3"], wn["w3"], 128, extra=0.5)

        def transpose_weight(nm):
            O, I = wn[nm].shape
            for j in range(I // 128):
                for o0 in range(0, O, 2048):
                    w = min(2048, O - o0)
                    sync.dma_start(out=stg[:, 0:w],
                                   in_=wn[nm].ap()[o0:o0 + w, j * 128:(j + 1) * 128],
                                   transpose=True)
                    if nm == "w3":
                        for h0 in range(0, w, 1024):
                            sync.dma_start(out=wt[nm].ap()[j // 2, :, j % 2,
                                                           0:1024] if w == 1024 else
                                           wt[nm].ap()[j // 2, :, j % 2, :],
                                           in_=stg[:, h0:h0 + 1024])
                    else:
                        for gg in range(w // 512):
                            g = (o0 // 512) + gg
                            sync.dma_start(out=wt[nm].ap()[j // 2, g, :, j % 2, :],
                                           in_=stg[:, gg * 512:(gg + 1) * 512])

        for nm in ("w1", "w2", "h1", "h2", "w3"):
            transpose_weight(nm)

        for dst, src in ((tri16, tri16_e), (trirep, trirep_e),
                         (sixcol, sixcol_e), (onesrow, onesrow_e), (ident, ident_e),
                         (bgk_bc, bgk_e), (gam_bc, gam_e), (bet_bc, bet_e)):
            sync.dma_start(out=dst[:, :], in_=src.ap())
        dv.memset(S[:, :], 0.0)

        def ln_to(dst_ap, src_ap, ev_guard=False):
            def f(e):
                e.bn_stats(stat[:, 0:6], src_ap[:, 0:512])
                return e.bn_stats(stat[:, 6:12], src_ap[:, 512:1024])
            if ev_guard:
                dv.waitsem(ev_sem, rec.ev)
            dv.group(f)
            dv.bn_aggr(mv[:, :], stat[:, :])
            dv.tensor_scalar_add(rstd[:, :], mv[:, 1:2], EPS)
            ac.activation(rstd[:, :], rstd[:, :], AF.Ln)
            ac.activation(rstd[:, :], rstd[:, :], AF.Exp, scale=-0.5)
            dv.scalar_tensor_tensor(nmean[:, :], mv[:, 0:1], -1.0, rstd[:, :], ALU.mult, ALU.mult)
            dv.tensor_scalar(dst_ap, src_ap, rstd[:, 0:1], nmean[:, 0:1], ALU.mult, ALU.add)

        def transpose_chunk(fm_sb, tok_f32, cc):
            def f(e):
                last = None
                for j in range(8):
                    last = e.transpose(psS[:, j * 128:(j + 1) * 128],
                                       tok_f32[:, j * 128:(j + 1) * 128], ident[:, :])
                return last
            pe.group(f)
            dv.tensor_copy(
                fm_sb[:, :].rearrange("p (j t) -> p j t", j=8)[:, :, cc * 128:(cc + 1) * 128],
                psS[:, :].rearrange("p (j c) -> p j c", j=8))

        def aproj(fm_sb, cc, wext, ps):
            for j2 in range(4):
                j0 = 2 * j2
                slot, pre, post = astr.load(
                    asl[:, (astr.n % 3) * 2 * D:(astr.n % 3 + 1) * 2 * D],
                    wext.ap()[j2, :, :, :].rearrange("p kk c -> p (kk c)"))

                def f(e, _sl=slot, _j0=j0):
                    last = None
                    for kk in range(2):
                        j = _j0 + kk
                        base = _sl * 2 * D + kk * D
                        lhsT = fm_sb[:, j * T + cc * 128: j * T + (cc + 1) * 128]
                        e.matmul(ps[:, 0:512], lhsT, asl[:, base:base + 512],
                                 start=(j == 0), stop=(j == 7))
                        last = e.matmul(ps[:, 512:1024], lhsT, asl[:, base + 512:base + 1024],
                                        start=(j == 0), stop=(j == 7))
                    return last
                pe.group(f, pre=(pre,), post=(post,))

        def wout_mm(cc, ps):
            for j2 in range(4):
                j0 = 2 * j2
                slot, pre, post = astr.load(
                    asl[:, (astr.n % 3) * 2 * D:(astr.n % 3 + 1) * 2 * D],
                    wa["Wout"].ap()[j2, :, :, :].rearrange("p kk c -> p (kk c)"))

                def f(e, _sl=slot, _j0=j0):
                    last = None
                    for kk in range(2):
                        j = _j0 + kk
                        base = _sl * 2 * D + kk * D
                        lhsT = yfm[:, j * 128:(j + 1) * 128]
                        e.matmul(ps[:, 0:512], lhsT, asl[:, base:base + 512],
                                 start=(j == 0), stop=(j == 7))
                        last = e.matmul(ps[:, 512:1024], lhsT, asl[:, base + 512:base + 1024],
                                        start=(j == 0), stop=(j == 7))
                    return last
                pe.group(f, pre=(pre,), post=(post,))

        def mm2S(lhsT, rhs0, rhs1, rows=(0, 128)):
            def f(e):
                e.matmul(psS[rows[0]:rows[1], 0:512], lhsT, rhs0, start=True, stop=True)
                return e.matmul(psS[rows[0]:rows[1], 512:1024], lhsT, rhs1, start=True, stop=True)
            pe.group(f)

        FLUSH_PER_GROUP = 6

        def stream_layer(wtname, nI, rhs_sb, evac):
            wte = wt[wtname]
            for g in range(wte.shape[1]):
                for k2 in range(nI // 2):
                    k0 = 2 * k2
                    slot, pre, post = wstr.load(
                        wsl[:, (wstr.n % 5) * 1024:(wstr.n % 5 + 1) * 1024],
                        wte.ap()[k2, g, :, :, :].rearrange("p kk c -> p (kk c)"))
                    evneed = rec.ev

                    def f(e, _sl=slot, _k0=k0, _nI=nI, _rhs=rhs_sb, _ev=evneed):
                        if _ev and _k0 == 0:
                            e.wait_ge(ev_sem, _ev)
                        last = None
                        for kk in range(2):
                            k = _k0 + kk
                            for u in range(4):
                                last = e.matmul(
                                    ps4b[:, u * 512:(u + 1) * 512],
                                    wsl[:, _sl * 1024 + kk * 512 + u * 128:
                                        _sl * 1024 + kk * 512 + (u + 1) * 128],
                                    _rhs[:, k * T:(k + 1) * T],
                                    start=(k == 0), stop=(k == _nI - 1))
                        return last
                    pe.group(f, pre=(pre,), post=(post,), tick=False, tkwait=False)
                    rec.flush(1)
                evac(g)
                rec.flush(FLUSH_PER_GROUP)

        def ffn_pass(src_fm, out_cb):
            def ev_w1(g):
                evp(ac, lambda e, _g=g: e.activation(
                    h_a[:, 4 * _g * T:(4 * _g + 4) * T], ps4b[:, :], AF.Silu),
                    pre=((wstr.free_sem, wstr.n),))
            stream_layer("w1", 8, src_fm, ev_w1)

            def ev_w2(g):
                evp(dv, lambda e, _g=g: e.tensor_mul(
                    h_b[:, 4 * _g * T:(4 * _g + 4) * T], ps4b[:, :],
                    h_a[:, 4 * _g * T:(4 * _g + 4) * T]),
                    pre=((wstr.free_sem, wstr.n), (ev_sem, rec.ev)))
            stream_layer("w2", 8, src_fm, ev_w2)

            def ev_h1(g):
                evp(ac, lambda e, _g=g: e.activation(
                    h_a[:, 4 * _g * T:(4 * _g + 4) * T], ps4b[:, :], AF.Gelu),
                    pre=((wstr.free_sem, wstr.n),))
            stream_layer("h1", NKH, h_b, ev_h1)

            def ev_h2(g):
                evp(ac, lambda e, _g=g: e.activation(
                    h_b[:, 4 * _g * T:(4 * _g + 4) * T], ps4b[:, :], AF.Gelu),
                    pre=((wstr.free_sem, wstr.n),))
            stream_layer("h2", NKH, h_a, ev_h2)

            for cc in range(NCH):
                for m2 in range(NKH // 2):
                    m0 = 2 * m2
                    slot, pre, post = w3str.load(
                        w3sl[:, (w3str.n % 3) * 2 * D:(w3str.n % 3 + 1) * 2 * D],
                        wt["w3"].ap()[m2, :, :, :].rearrange("p kk c -> p (kk c)"))
                    evneed = rec.ev

                    def f(e, _sl=slot, _m0=m0, _cc=cc, _ev=evneed):
                        if _ev and _m0 == 0:
                            e.wait_ge(ev_sem, _ev)
                        last = None
                        for kk in range(2):
                            m = _m0 + kk
                            base = _sl * 2 * D + kk * D
                            lhsT = h_b[:, m * T + _cc * 128: m * T + (_cc + 1) * 128]
                            e.matmul(psD[:, 0:512], lhsT, w3sl[:, base:base + 512],
                                     start=(m == 0), stop=(m == NKH - 1))
                            last = e.matmul(psD[:, 512:1024], lhsT, w3sl[:, base + 512:base + 1024],
                                            start=(m == 0), stop=(m == NKH - 1))
                        return last
                    pe.group(f, pre=(pre,), post=(post,), tick=False, tkwait=False)
                out_cb(cc)
                rec.flush(2 * FLUSH_PER_GROUP)

        def scan_chunk(cc, own, strip):
            tok0 = strip * T + cc * 128
            gp.dma_start(out=r1d[:, :], in_=r1sc.ap()[tok0:tok0 + 128, :])
            ln_to(xnt[:, :], r1d[:, :])
            transpose_chunk(xn2_fm, xnt, cc)

            def f_z1(e):
                last = None
                for j in range(8):
                    last = e.matmul(psS[0:16, 0:128], agk1_sb[:, j * 16:(j + 1) * 16],
                                    xn2_fm[:, j * T + cc * 128: j * T + (cc + 1) * 128],
                                    start=(j == 0), stop=(j == 7))
                return last
            pe.group(f_z1)
            dv.tensor_copy(z1_bf[:, :], psS[0:16, 0:128])
            mm2S(z1_bf[:, :], agk2_sb[:, 0:512], agk2_sb[:, 512:1024])
            dv.tensor_add(zb[:, :], psS[:, :], bgk_bc[:, :])
            ac.activation(t1[:, :], zb[:, :], AF.Abs)
            ac.activation(t2[:, :], t1[:, :], AF.Exp, scale=-1.0)
            dv.tensor_scalar_add(t1[:, :], t2[:, :], 1.0)
            ac.activation(t2[:, :], t1[:, :], AF.Ln)
            dv.tensor_scalar_min(t1[:, :], zb[:, :], 0.0)
            dv.tensor_sub(g_bf[:, :], t1[:, :], t2[:, :])

            mm2S(tri16[:, :], g_bf[:, 0:512], g_bf[:, 512:1024])
            dv.tensor_copy(cumG[:, :], psS[:, :])
            dv.tensor_copy(mid_bf[:, :], cumG[64:65, :])

            def f_sumg(e):
                e.matmul(psS[0:1, 0:512], sixcol[:, :], g_bf[:, 0:512], start=True, stop=True)
                return e.matmul(psS[0:1, 512:1024], sixcol[:, :], g_bf[:, 512:1024],
                                start=True, stop=True)
            pe.group(f_sumg)
            dv.tensor_sub(xnt[0:1, :], psS[0:1, :], mid_bf[:, :])
            mm2S(onesrow[0:1, :], mid_bf[0:1, 0:512], mid_bf[0:1, 512:1024])
            dv.tensor_sub(cumGc[:, :], cumG[:, :], psS[:, :])
            ac.activation(lam[:, :], cumGc[:, :], AF.Exp)
            ac.activation(dinv[:, :], cumGc[:, :], AF.Exp, scale=-1.0)
            ac.activation(emu_bf[:, :], mid_bf[:, :], AF.Exp)
            ac.activation(f_bf[:, :], xnt[0:1, :], AF.Exp)

            mm2S(onesrow[0:1, 0:64], emu_bf[0:1, 0:512], emu_bf[0:1, 512:1024], rows=(0, 64))
            dv.tensor_mul(S0x[0:64, :], S[:, :], psS[0:64, :])
            dv.tensor_copy(S0x_bf[:, :], S0x[0:64, :])

            aproj(xn2_fm, cc, wa["Wv"], psS)
            dv.tensor_mul(vt_bf[:, :], psS[:, :], dinv[:, :])
            aproj(xn2_fm, cc, wa["Wk"], psS)
            dv.tensor_copy(ktok[:, :], psS[:, :])
            if own:
                dv.tensor_copy(ktf[:, :], psS[:, :])

            def f_ds(e):
                last = None
                for h in range(NH):
                    last = e.matmul(psS[0:64, h * 64:(h + 1) * 64],
                                    ktok[:, h * 64:(h + 1) * 64], vt_bf[:, h * 64:(h + 1) * 64],
                                    start=(h % 8 == 0), stop=(h % 8 == 7))
                return last
            pe.group(f_ds)
            dv.tensor_add(S[:, :], S0x[0:64, :], psS[0:64, :])
            mm2S(onesrow[0:1, 0:64], f_bf[0:1, 0:512], f_bf[0:1, 512:1024], rows=(0, 64))
            dv.tensor_mul(S[:, :], S[:, :], psS[0:64, :])

            if not own:
                return

            def tr8(e, src_sb):
                last = None
                for j in range(8):
                    last = e.transpose(psS[:, j * 128:(j + 1) * 128],
                                       src_sb[:, j * 128:(j + 1) * 128], ident[:, :])
                return last
            pe.group(lambda e: tr8(e, ktf))
            dv.tensor_copy(kfm[0:64, 0:1024].rearrange("p (j c) -> p j c", j=8),
                           psS[0:64, :].rearrange("p (j c) -> p j c", j=8))
            dv.tensor_copy(kfm[0:64, 1024:2048].rearrange("p (j c) -> p j c", j=8),
                           psS[64:128, :].rearrange("p (j c) -> p j c", j=8))
            aproj(xn2_fm, cc, wa["Wq"], psS)
            dv.tensor_copy(qtf[:, :], psS[:, :])
            pe.group(lambda e: tr8(e, qtf))
            dv.tensor_copy(qfm[0:64, 0:1024].rearrange("p (j c) -> p j c", j=8),
                           psS[0:64, :].rearrange("p (j c) -> p j c", j=8))
            dv.tensor_copy(qfm[0:64, 1024:2048].rearrange("p (j c) -> p j c", j=8),
                           psS[64:128, :].rearrange("p (j c) -> p j c", j=8))

            for q4 in range(4):
                def f_a(e, _q=q4):
                    last = None
                    for hh in range(4):
                        h = _q * 4 + hh
                        b = (h >> 1) + 8 * (h & 1)
                        last = e.matmul(psS[:, hh * 128:(hh + 1) * 128],
                                        kfm[0:64, b * 128:(b + 1) * 128],
                                        qfm[0:64, b * 128:(b + 1) * 128],
                                        start=(hh == 0), stop=(hh == 3))
                    return last
                pe.group(f_a)
                dv.tensor_mul(am[:, :], psS[:, 0:512], trirep[:, :])

                def f_oi(e, _q=q4):
                    last = None
                    for hh in range(4):
                        h = _q * 4 + hh
                        b = (h >> 1) + 8 * (h & 1)
                        col = 512 + 64 * hh
                        e.matmul(psS[:, col:col + 64], am[:, hh * 128:(hh + 1) * 128],
                                 vt_bf[:, h * 64:(h + 1) * 64],
                                 start=(hh == 0), stop=False)
                        last = e.matmul(psS[:, col:col + 64],
                                        qfm[0:64, b * 128:(b + 1) * 128],
                                        S0x_bf[0:64, h * 64:(h + 1) * 64],
                                        start=False, stop=(hh == 3))
                    return last
                pe.group(f_oi)
                dv.tensor_mul(out_pre[:, q4 * 256:(q4 + 1) * 256], psS[:, 512:768],
                              lam[:, q4 * 256:(q4 + 1) * 256])

            ln_to(t1[:, :], out_pre[:, :])
            dv.tensor_mul(t2[:, :], t1[:, :], gam_bc[:, :])
            dv.tensor_add(t2[:, :], t2[:, :], bet_bc[:, :])
            aproj(xn2_fm, cc, wa["Wg"], psS)
            ac.activation(t1[:, :], psS[:, :], AF.Silu)
            dv.tensor_mul(ytf[:, :], t2[:, :], t1[:, :])
            pe.group(lambda e: tr8(e, ytf))
            dv.tensor_copy(yfm[:, :].rearrange("p (j c) -> p j c", j=8),
                           psS[:, :].rearrange("p (j c) -> p j c", j=8))
            wout_mm(cc, psS)
            own_tok0 = (strip - NPRE) * T + cc * 128
            gp.dma_start(out=r1d[:, :], in_=r1sc.ap()[tok0:tok0 + 128, :])
            dv.tensor_add(xnt[:, :], r1d[:, :], psS[:, :])
            gp.dma_start(out=r2sc.ap()[own_tok0:own_tok0 + 128, :], in_=xnt[:, :])

        def record_ffn1(s):
            gp.group(lambda e, _s=s: e.dma_start(
                out=x_sb[:, :].rearrange("p (c d) -> p c d", c=NCH),
                in_=xin.ap().rearrange("(s c p) d -> p s c d", s=NSTRIP, p=128)[:, _s, :, :]),
                pre=((ev_sem, rec.ev),))
            for cc in range(NCH):
                ln_to(xnt[:, :], x_sb[:, cc * D:(cc + 1) * D],
                      ev_guard=(cc == 0))
                transpose_chunk(xn_fm, xnt, cc)

            def out1(cc, _s=s):
                dv.waitsem(w3str.free_sem, w3str.n)
                evp(dv, lambda e, _cc=cc: e.tensor_add(
                    r1c[:, :], psD[:, :], x_sb[:, _cc * D:(_cc + 1) * D]))
                tok0 = _s * T + cc * 128
                gp.group(lambda e, _t=tok0: e.dma_start(
                    out=r1sc.ap()[_t:_t + 128, :], in_=r1c[:, :]),
                    pre=((ev_sem, rec.ev),))

            ffn_pass(xn_fm, out1)

        def record_ffn2(s):
            for cc in range(NCH):
                own_tok0 = (s - NPRE) * T + cc * 128
                gp.group(lambda e, _t=own_tok0: e.dma_start(
                    out=r1c[:, :], in_=r2sc.ap()[_t:_t + 128, :]),
                    pre=((ev_sem, rec.ev),))
                ln_to(xnt[:, :], r1c[:, :], ev_guard=(cc == 0))
                transpose_chunk(xn_fm, xnt, cc)

            def out2(cc, _s=s):
                dv.waitsem(w3str.free_sem, w3str.n)
                evp(dv, lambda e: e.tensor_copy(w3tmp[:, :], psD[:, :]))
                own_tok0 = (_s - NPRE) * T + cc * 128
                gp.group(lambda e, _t=own_tok0: e.dma_start(
                    out=r1c[:, :], in_=r2sc.ap()[_t:_t + 128, :]),
                    pre=((ev_sem, rec.ev),))
                evp(dv, lambda e: e.tensor_add(xnt[:, :], r1c[:, :], w3tmp[:, :]),
                    pre=((ev_sem, rec.ev),))
                gp.group(lambda e, _t=own_tok0: e.dma_start(
                    out=out_ext.ap()[_t:_t + 128, :], in_=xnt[:, :]),
                    pre=((ev_sem, rec.ev),))

            ffn_pass(xn_fm, out2)

        prev_own = False
        for s in range(NSTRIP):
            own = s >= NPRE
            record_ffn1(s)
            rec.flush_all()
            if prev_own:
                record_ffn2(s - 1)
            rec.defer = True
            for cc in range(NCH):
                scan_chunk(cc, own, s)
            rec.defer = False
            prev_own = own
        rec.flush_all()
        record_ffn2(NSTRIP - 1)
        rec.flush_all()

        final_ticket = rec.val
        final_ev = rec.ev

        def replay(engname):
            skip_self = engname == "pe"

            def f(eng):
                low = 0
                qcnt = 0
                for (w, fn, inc, pre, post) in rec.items[engname]:
                    if w is not None and w > low:
                        eng.wait_ge(tk_sem, w)
                        low = w
                    for (sem, val) in pre:
                        eng.wait_ge(sem, val)
                    inst = fn(eng)
                    if inc == 16:
                        qs = qsems[engname]
                        assert inst is not None
                        inst.then_inc(qs, 16)
                        qcnt += 16
                        eng.wait_ge(qs, qcnt)
                        eng.nop().then_inc(tk_sem, 16)
                        for (sem, val) in post:
                            eng.nop().then_inc(sem, val)
                    elif inc or post:
                        assert inst is not None, f"group must return an instruction ({engname})"
                        updates = ([(tk_sem, inc)] if inc else []) + list(post)
                        inst.then_inc(*updates[0])
                        for (sem, val) in updates[1:]:
                            eng.nop().then_inc(sem, val)
                    if skip_self:
                        low += inc
                if engname == "gp":
                    eng.wait_ge(tk_sem, final_ticket)
                    eng.wait_ge(ev_sem, final_ev)
            return f

        block.sync(replay("sync"))
        block.gpsimd(replay("gp"))
        block.tensor(replay("pe"))
        block.vector(replay("dv"))
        block.scalar(replay("ac"))
        for k in list(rec.items):
            rec.items[k] = []
    return nc


def _consts(cfg):
    import ml_dtypes
    tri = np.tril(np.ones((128, 128), dtype=np.float32)).T
    bf = lambda a: a.astype(ml_dtypes.bfloat16)
    return dict(
        tri16_c=bf(tri / 16.0),
        trirep_c=bf(np.tile(tri, (1, 4))),
        sixcol_c=bf(np.full((128, 1), 1 / 16.0, dtype=np.float32)),
        onesrow_c=bf(np.ones((1, 128), dtype=np.float32)),
        ident_c=np.eye(128, dtype=np.float32),
    )


def make_in_maps(cfg, x_full, W):
    T, NSTRIP, NOWN = cfg["T"], cfg["n_strips"], cfg["own_strips"]
    SEQ = NSTRIP * T
    SLAB = NOWN * T
    consts = _consts(cfg)
    f32 = np.float32

    def fm_scales(s):
        return np.ascontiguousarray(s.reshape(-1, 128).T.astype(f32))

    scales = np.concatenate([fm_scales(W["ffn_w1_s"]), fm_scales(W["ffn_w2_s"]),
                             fm_scales(W["ffn_h1_s"]), fm_scales(W["ffn_h2_s"]),
                             fm_scales(W["ffn_w3_s"])], axis=1)
    bcast = lambda v: np.ascontiguousarray(np.broadcast_to(v.astype(f32), (128, D)))
    shared = dict(
        Wq=W["attn_Wq"], Wk=W["attn_Wk"], Wv=W["attn_Wv"], Wg=W["attn_Wg"],
        Wout=W["attn_Wout"], Wgk1=W["attn_Wgk1"], Wgk2=W["attn_Wgk2"],
        w1=W["ffn_w1_w"], w2=W["ffn_w2_w"], h1=W["ffn_h1_w"], h2=W["ffn_h2_w"],
        w3=W["ffn_w3_w"],
        scales=scales,
        bgk_bc=bcast(W["attn_bgk2"]),
        gam_bc=bcast(W["attn_gamma"]),
        bet_bc=bcast(W["attn_beta"]),
        **consts,
    )
    in_maps = []
    for core in range(8):
        b, p = divmod(core, 4)
        take = (p + 1) * SLAB
        xi = np.zeros((SEQ, D), dtype=f32)
        xi[SEQ - take:] = x_full[b, :take]
        m = dict(shared)
        m["xin"] = xi
        in_maps.append(m)
    return in_maps


def kernel(**inputs):
    cfg = CFG_FULL
    W = {
        "attn_Wq": np.asarray(inputs["L0_attn_Wq"]),
        "attn_Wk": np.asarray(inputs["L0_attn_Wk"]),
        "attn_Wv": np.asarray(inputs["L0_attn_Wv"]),
        "attn_Wg": np.asarray(inputs["L0_attn_Wg"]),
        "attn_Wout": np.asarray(inputs["L0_attn_Wout"]),
        "attn_Wgk1": np.asarray(inputs["L0_attn_Wgk1"]),
        "attn_Wgk2": np.asarray(inputs["L0_attn_Wgk2"]),
        "attn_bgk2": np.asarray(inputs["L0_attn_bgk2"]),
        "attn_gamma": np.asarray(inputs["L0_attn_gamma"]),
        "attn_beta": np.asarray(inputs["L0_attn_beta"]),
        "ffn_w1_w": np.asarray(inputs["L0_ffn_w1_w"]),
        "ffn_w1_s": np.asarray(inputs["L0_ffn_w1_s"]),
        "ffn_w2_w": np.asarray(inputs["L0_ffn_w2_w"]),
        "ffn_w2_s": np.asarray(inputs["L0_ffn_w2_s"]),
        "ffn_h1_w": np.asarray(inputs["L0_ffn_h1_w"]),
        "ffn_h1_s": np.asarray(inputs["L0_ffn_h1_s"]),
        "ffn_h2_w": np.asarray(inputs["L0_ffn_h2_w"]),
        "ffn_h2_s": np.asarray(inputs["L0_ffn_h2_s"]),
        "ffn_w3_w": np.asarray(inputs["L0_ffn_w3_w"]),
        "ffn_w3_s": np.asarray(inputs["L0_ffn_w3_s"]),
    }
    x_full = np.asarray(inputs["L0_attn_x"], dtype=np.float32)
    in_maps = make_in_maps(cfg, x_full, W)
    del x_full, W
    import gc
    gc.collect()
    nc = build_program(cfg)
    res = run_bass_kernel_spmd(nc, in_maps, core_ids=list(range(8)))
    SLAB = cfg["own_strips"] * cfg["T"]
    out = np.empty((2, 16384, D), dtype=np.float32)
    for core in range(8):
        b, p = divmod(core, 4)
        out[b, p * SLAB:(p + 1) * SLAB] = np.asarray(res.results[core]["out"])
    return out


# version 14
# speedup vs baseline: 1.3085x; 1.2218x over previous
import numpy as np
import concourse.bass as bass
import concourse.mybir as mybir
from concourse.bass_utils import run_bass_kernel_spmd
from contextlib import ExitStack

F32 = mybir.dt.float32
BF16 = mybir.dt.bfloat16
AF = mybir.ActivationFunctionType
ALU = mybir.AluOpType

D = 1024
DFF = 4096
NH, HD = 16, 64
EPS = 1e-5

CFG_FULL = dict(T=512, n_strips=32, own_strips=8)

ENGS = ("sync", "gp", "pe", "dv", "ac")


class Rec:
    def __init__(self):
        self.items = {k: [] for k in ENGS}
        self.val = 0
        self.ev = 0
        self.pending = []
        self.defer = False

    def emit(self, eng, fn, inc, pre=(), post=(), evflag=False, tkwait=True):
        if self.defer:
            self.pending.append((eng, fn, inc, tuple(pre), tuple(post), evflag, tkwait))
            return
        self._commit(eng, fn, inc, pre, post, evflag, tkwait)

    def _commit(self, eng, fn, inc, pre, post, evflag, tkwait=True):
        if evflag:
            self.ev += 1
        self.items[eng].append((self.val if tkwait else None, fn, inc,
                                tuple(pre), tuple(post)))
        self.val += inc

    def flush(self, k):
        n = 0
        while self.pending and n < k:
            item = self.pending.pop(0)
            self._commit(*item)
            n += 1

    def flush_all(self):
        self.flush(len(self.pending))


class EngRec:
    def __init__(self, rec, name, inc):
        object.__setattr__(self, "_rec", rec)
        object.__setattr__(self, "_name", name)
        object.__setattr__(self, "_inc", inc)

    def __getattr__(self, meth):
        rec, name, inc = self._rec, self._name, self._inc

        def call(*args, **kwargs):
            rec.emit(name, lambda e: getattr(e, meth)(*args, **kwargs), inc)

        return call

    def group(self, fn, pre=(), post=(), tick=True, evflag=False, tkwait=True):
        self._rec.emit(self._name, fn, self._inc if tick else 0, pre, post,
                       evflag, tkwait)

    def waitsem(self, sem, val):
        self._rec.emit(self._name, lambda e: e.wait_ge(sem, val), 0)


class Stream:

    def __init__(self, rec, nc, stack, name, qnames, nbuf=2, gate=None):
        self.rec = rec
        self.qnames = qnames
        self.nbuf = nbuf
        self.load_sems = [stack.enter_context(nc.semaphore(f"{name}_ld{i}"))
                          for i in range(nbuf)]
        self.free_sem = stack.enter_context(nc.semaphore(f"{name}_fr"))
        self.n = 0
        self.gate = gate
        self.gwait = {q: -1 for q in qnames}

    def load(self, out_ap, in_ap):
        slot = self.n % self.nbuf
        need_free = self.n - self.nbuf + 1
        ls, fs = self.load_sems[slot], self.free_sem
        q = self.qnames[self.n % len(self.qnames)]
        gw = None
        if self.gate is not None:
            gsem, gcnt = self.gate
            if self.gwait[q] < gcnt[0]:
                gw = (gsem, gcnt[0])
                self.gwait[q] = gcnt[0]

        def fn(e, _nf=need_free, _o=out_ap, _i=in_ap, _gw=gw):
            if _gw is not None:
                e.wait_ge(*_gw)
            if _nf > 0:
                e.wait_ge(fs, _nf)
            return e.dma_start(out=_o, in_=_i)

        self.rec.emit(q, fn, 0, (), ((ls, 16),), tkwait=self.gate is None)
        self.n += 1
        nuse = (self.n - 1) // self.nbuf + 1
        return slot, (ls, 16 * nuse), (fs, 1)


class PrepPipe:

    def __init__(self, rec, nc, stack, name, tens, slot_cols, gate):
        self.rec = rec
        self.tens = tens
        self.offs = [0, slot_cols]
        self.ld = [stack.enter_context(nc.semaphore(f"{name}_ld{i}"))
                   for i in range(2)]
        self.wc = stack.enter_context(nc.semaphore(f"{name}_wc"))
        self.gate = gate
        self.n = 0
        self.wn = 0
        self.gwait = -1
        self.hist = []

    def move(self, qr, qw, in_ap, outs, cols, transpose=False, gated_read=False):
        slot = self.n % 2
        off = self.offs[slot]
        ls = self.ld[slot]
        gsem, gcnt = self.gate
        gw = None
        if gated_read and self.gwait < gcnt[0]:
            gw = (gsem, gcnt[0])
            self.gwait = gcnt[0]
        if self.n >= 2:
            fw = (gsem, self.hist[self.n - 2])
            if gw is not None:
                fw = (gsem, max(gw[1], fw[1]))
            gw = fw
        nuse = self.n // 2 + 1
        sview = self.tens[:, off:off + cols]

        def rf(e, _i=in_ap, _s=sview, _gw=gw, _t=transpose):
            if _gw is not None:
                e.wait_ge(*_gw)
            if _t:
                return e.dma_start(out=_s, in_=_i, transpose=True)
            return e.dma_start(out=_s, in_=_i)

        self.rec.emit(qr, rf, 0, (), ((ls, 16),), tkwait=False)
        self.n += 1
        for oap, (c0, c1) in outs:
            self.wn += 16

            def wf(e, _o=oap, _s=self.tens[:, off + c0:off + c1], _v=16 * nuse,
                   _ls=ls):
                e.wait_ge(_ls, _v)
                return e.dma_start(out=_o, in_=_s)

            def cf(e, _w=self.wn):
                e.wait_ge(self.wc, _w)
                return e.nop()

            self.rec.emit(qw, wf, 0, (), ((self.wc, 16),), tkwait=False)
            self.rec.emit(qw, cf, 0, (), ((gsem, 16),), tkwait=False)
            gcnt[0] += 16
        self.hist.append(gcnt[0])


def build_program(cfg):
    T = cfg["T"]
    NSTRIP = cfg["n_strips"]
    NOWN = cfg["own_strips"]
    NPRE = NSTRIP - NOWN
    SEQ = NSTRIP * T
    NCH = T // 128
    OWN_TOK = NOWN * T
    NKH = DFF // 128

    nc = bass.Bass("TRN2", target_bir_lowering=False)

    xin = nc.dram_tensor("xin", [SEQ, D], F32, kind="ExternalInput")
    wq_e = nc.dram_tensor("Wq", [D, D], F32, kind="ExternalInput")
    wk_e = nc.dram_tensor("Wk", [D, D], F32, kind="ExternalInput")
    wv_e = nc.dram_tensor("Wv", [D, D], F32, kind="ExternalInput")
    wg_e = nc.dram_tensor("Wg", [D, D], F32, kind="ExternalInput")
    wo_e = nc.dram_tensor("Wout", [D, D], F32, kind="ExternalInput")
    wgk1_e = nc.dram_tensor("Wgk1", [D, 16], F32, kind="ExternalInput")
    wgk2_e = nc.dram_tensor("Wgk2", [16, D], F32, kind="ExternalInput")
    ffn_e = {nm: nc.dram_tensor(nm, shape, F32, kind="ExternalInput")
             for nm, shape in (("w1", [DFF, D]), ("w2", [DFF, D]), ("h1", [DFF, DFF]),
                               ("h2", [DFF, DFF]), ("w3", [D, DFF]))}
    scales_e = nc.dram_tensor("scales", [128, 136], F32, kind="ExternalInput")
    bgk_e = nc.dram_tensor("bgk_bc", [128, D], F32, kind="ExternalInput")
    gam_e = nc.dram_tensor("gam_bc", [128, D], F32, kind="ExternalInput")
    bet_e = nc.dram_tensor("bet_bc", [128, D], F32, kind="ExternalInput")
    tri16_e = nc.dram_tensor("tri16_c", [128, 128], BF16, kind="ExternalInput")
    trirep_e = nc.dram_tensor("trirep_c", [128, 512], BF16, kind="ExternalInput")
    sixcol_e = nc.dram_tensor("sixcol_c", [128, 1], BF16, kind="ExternalInput")
    onesrow_e = nc.dram_tensor("onesrow_c", [1, 128], BF16, kind="ExternalInput")
    ident_e = nc.dram_tensor("ident_c", [128, 128], F32, kind="ExternalInput")
    out_ext = nc.dram_tensor("out", [OWN_TOK, D], F32, kind="ExternalOutput")

    wn = {nm: nc.dram_tensor(f"wn_{nm}", list(ffn_e[nm].shape), BF16) for nm in ffn_e}
    wt = {"w1": nc.dram_tensor("wt_w1", [4, 8, 128, 2, 512], BF16),
          "w2": nc.dram_tensor("wt_w2", [4, 8, 128, 2, 512], BF16),
          "h1": nc.dram_tensor("wt_h1", [16, 8, 128, 2, 512], BF16),
          "h2": nc.dram_tensor("wt_h2", [16, 8, 128, 2, 512], BF16),
          "w3": nc.dram_tensor("wt_w3", [16, 128, 2, 1024], BF16)}
    wa = {nm: nc.dram_tensor(f"wa_{nm}", [4, 128, 2, 1024], BF16)
          for nm in ("Wq", "Wk", "Wv", "Wg", "Wout")}
    r1sc = nc.dram_tensor("r1sc", [SEQ, D], F32)
    r2sc = nc.dram_tensor("r2sc", [OWN_TOK, D], F32)

    with ExitStack() as stack:
        E = stack.enter_context

        def sb(name, shape, dt):
            return E(nc.sbuf_tensor(name, shape, dt))

        h_a = sb("h_a", [128, NKH * T], BF16)
        h_b = sb("h_b", [128, NKH * T], BF16)
        x_sb = sb("x_sb", [128, NCH * D], F32)
        xn_fm = sb("xn_fm", [128, 8 * T], BF16)
        xn2_fm = sb("xn2_fm", [128, 8 * T], BF16)
        xnt = sb("xnt", [128, D], F32)
        r1c = sb("r1c", [128, D], F32)
        wsl = sb("wsl", [128, 5 * 1024], BF16)
        w3sl = sb("w3sl", [128, 3 * 2 * D], BF16)
        asl = sb("asl", [128, 3 * 2 * D], BF16)
        agk1_sb = sb("agk1_sb", [128, 8 * 16], BF16)
        agk2_sb = sb("agk2_sb", [16, D], BF16)
        scales = sb("scales_sb", [128, 136], F32)
        tri16 = sb("tri16", [128, 128], BF16)
        trirep = sb("trirep", [128, 512], BF16)
        sixcol = sb("sixcol", [128, 1], BF16)
        onesrow = sb("onesrow", [1, 128], BF16)
        ident = sb("ident", [128, 128], F32)
        bgk_bc = sb("bgk_bc_s", [128, D], F32)
        gam_bc = sb("gam_bc_s", [128, D], F32)
        bet_bc = sb("bet_bc_s", [128, D], F32)
        S = sb("S", [64, D], F32)
        S0x_bf = sb("S0x_bf", [64, D], BF16)
        zb = sb("zb", [128, D], F32)
        t1 = sb("t1", [128, D], F32)
        t2 = sb("t2", [128, D], F32)
        cumG = sb("cumG", [128, D], F32)
        g_bf = sb("g_bf", [128, D], BF16)
        vt_bf = sb("vt_bf", [128, D], BF16)
        ktok = sb("ktok", [128, D], BF16)
        ktf = sb("ktf", [128, D], F32)
        kfm = sb("kfm", [64, 2 * D], BF16)
        qfm = sb("qfm", [64, 2 * D], BF16)
        am = sb("am", [128, 512], BF16)
        z1_bf = sb("z1_bf", [16, 128], BF16)
        mid_bf = sb("mid_bf", [1, D], BF16)
        emu_bf = sb("emu_bf", [1, D], BF16)
        stat = sb("stat", [128, 12], F32)
        mv = sb("mv", [128, 2], F32)
        rstd = sb("rstd", [128, 1], F32)
        nmean = sb("nmean", [128, 1], F32)
        stg = sb("stg", [128, 2048], BF16)
        cstg = w3sl
        tstg = asl
        stgf = t1

        out_pre = zb
        r1d = zb
        lam = t1
        dinv = t2
        cumGc = cumG
        S0x = cumG
        f_bf = mid_bf
        yfm = g_bf
        qtf = ktf
        ytf = ktf
        w3tmp = ktf

        ps4b = E(nc.psum_tensor("ps4b", [128, 2048], F32))
        psD = E(nc.psum_tensor("psD", [128, 1024], F32))
        psS = E(nc.psum_tensor("psS", [128, 1024], F32))

        tk_sem = E(nc.semaphore("tk"))
        ev_sem = E(nc.semaphore("ev"))
        prep_sem = E(nc.semaphore("prep"))
        xn_sem = E(nc.semaphore("xn"))
        XN = [0]
        qsems = {q: E(nc.semaphore(f"q_{q}")) for q in ("sync", "gp", "ac")}
        block = E(nc.Block())

        rec = Rec()
        sync = EngRec(rec, "sync", 16)
        gp = EngRec(rec, "gp", 16)
        pe = EngRec(rec, "pe", 1)
        dv = EngRec(rec, "dv", 1)
        ac = EngRec(rec, "ac", 1)

        PREPW = [0]
        GATE = (prep_sem, PREPW)
        wstr = Stream(rec, nc, stack, "wst", ("sync", "ac"), nbuf=5, gate=GATE)
        w3str = Stream(rec, nc, stack, "w3st", ("gp",), nbuf=3, gate=GATE)
        astr = Stream(rec, nc, stack, "ast", ("gp",), nbuf=3)

        def evp(engrec, fn, pre=()):
            engrec.group(fn, pre=pre, post=((ev_sem, 1),), tick=False, evflag=True)

        sync.dma_start(out=scales[:, :], in_=scales_e.ap())

        cpipe = PrepPipe(rec, nc, stack, "cp", cstg, 1024, GATE)
        tpipe = PrepPipe(rec, nc, stack, "tp", tstg, 1024, GATE)

        def prep_ffn_weight(nm):
            ext = ffn_e[nm]
            O, I = ext.shape
            for j in range(O // 128):
                for h0 in range(0, I, 1024):
                    cpipe.move(
                        "gp", "ac",
                        ext.ap()[j * 128:(j + 1) * 128, h0:h0 + 1024],
                        [(wn[nm].ap()[j * 128:(j + 1) * 128, h0:h0 + 1024],
                          (0, 1024))],
                        cols=1024)
            for jj in range(I // 128):
                for o0 in range(0, O, 1024):
                    outs = [(wt[nm].ap()[jj // 2, (o0 // 512) + gg, :, jj % 2, :],
                             (gg * 512, (gg + 1) * 512)) for gg in range(2)]
                    tpipe.move(
                        "sync", "ac",
                        wn[nm].ap()[o0:o0 + 1024, jj * 128:(jj + 1) * 128],
                        outs, cols=1024, transpose=True, gated_read=True)

        def cast_store(ext, scratch):
            for j in range(8):
                view = stg[:, 0:D]
                gp.dma_start(out=view, in_=ext.ap()[j * 128:(j + 1) * 128, :])
                sync.dma_start(out=scratch.ap()[j // 2, :, j % 2, :], in_=view)

        def cast_store_scaled(ext, scratch, scol0, extra=1.0):
            rows, cols = ext.shape
            nch = rows // 128
            for j in range(nch):
                for h0 in range(0, cols, 1024):
                    w = min(1024, cols - h0)
                    gp.dma_start(out=stgf[:, 0:w],
                                 in_=ext.ap()[j * 128:(j + 1) * 128, h0:h0 + w])
                    dv.tensor_scalar(stg[:, 0:w], stgf[:, 0:w],
                                     scales[:, scol0 + j:scol0 + j + 1], extra,
                                     ALU.mult, ALU.mult)
                    sync.dma_start(out=scratch.ap()[j * 128:(j + 1) * 128, h0:h0 + w],
                                   in_=stg[:, 0:w])

        def transpose_w3():
            for j in range(32):
                tpipe.move("sync", "ac",
                           wn["w3"].ap()[0:1024, j * 128:(j + 1) * 128],
                           [(wt["w3"].ap()[j // 2, :, j % 2, :], (0, 1024))],
                           cols=1024, transpose=True)

        def prep_rest():
            cast_store_scaled(ffn_e["w3"], wn["w3"], 128, extra=0.5)
            transpose_w3()
            for ext, nm in ((wq_e, "Wq"), (wk_e, "Wk"), (wv_e, "Wv"),
                            (wg_e, "Wg"), (wo_e, "Wout")):
                cast_store(ext, wa[nm])
            gp.dma_start(out=agk1_sb[:, :].rearrange("p (j o) -> p j o", j=8),
                         in_=wgk1_e.ap().rearrange("(j p) o -> p j o", p=128))
            gp.dma_start(out=agk2_sb[:, :], in_=wgk2_e.ap())

        for dst, src in ((tri16, tri16_e), (trirep, trirep_e),
                         (sixcol, sixcol_e), (onesrow, onesrow_e),
                         (ident, ident_e), (bgk_bc, bgk_e),
                         (gam_bc, gam_e), (bet_bc, bet_e)):
            sync.dma_start(out=dst[:, :], in_=src.ap())
        dv.memset(S[:, :], 0.0)

        prep_ffn_weight("w1")

        def prep_hook(i):
            if i == 1:
                prep_ffn_weight("w2")
            elif i == 2:
                prep_ffn_weight("h1")
            elif i == 3:
                prep_ffn_weight("h2")
            elif i == 4:
                prep_rest()

        def ln_to(dst_ap, src_ap, ev_guard=False):
            def f(e):
                e.bn_stats(stat[:, 0:6], src_ap[:, 0:512])
                return e.bn_stats(stat[:, 6:12], src_ap[:, 512:1024])
            if ev_guard:
                dv.waitsem(ev_sem, rec.ev)
            dv.group(f)
            dv.bn_aggr(mv[:, :], stat[:, :])
            dv.tensor_scalar_add(rstd[:, :], mv[:, 1:2], EPS)
            ac.activation(rstd[:, :], rstd[:, :], AF.Ln)
            ac.activation(rstd[:, :], rstd[:, :], AF.Exp, scale=-0.5)
            dv.scalar_tensor_tensor(nmean[:, :], mv[:, 0:1], -1.0, rstd[:, :], ALU.mult, ALU.mult)
            dv.tensor_scalar(dst_ap, src_ap, rstd[:, 0:1], nmean[:, 0:1], ALU.mult, ALU.add)

        def transpose_chunk(fm_sb, tok_f32, cc, bump=False):
            def f(e):
                last = None
                for j in range(8):
                    last = e.transpose(psS[:, j * 128:(j + 1) * 128],
                                       tok_f32[:, j * 128:(j + 1) * 128], ident[:, :])
                return last
            pe.group(f)

            def ev(e):
                return e.tensor_copy(
                    fm_sb[:, :].rearrange("p (j t) -> p j t", j=8)[:, :, cc * 128:(cc + 1) * 128],
                    psS[:, :].rearrange("p (j c) -> p j c", j=8))
            dv.group(ev)

        def aproj(fm_sb, cc, wext, ps):
            for j2 in range(4):
                j0 = 2 * j2
                slot, pre, post = astr.load(
                    asl[:, (astr.n % 3) * 2 * D:(astr.n % 3 + 1) * 2 * D],
                    wext.ap()[j2, :, :, :].rearrange("p kk c -> p (kk c)"))

                def f(e, _sl=slot, _j0=j0):
                    last = None
                    for kk in range(2):
                        j = _j0 + kk
                        base = _sl * 2 * D + kk * D
                        lhsT = fm_sb[:, j * T + cc * 128: j * T + (cc + 1) * 128]
                        e.matmul(ps[:, 0:512], lhsT, asl[:, base:base + 512],
                                 start=(j == 0), stop=(j == 7))
                        last = e.matmul(ps[:, 512:1024], lhsT, asl[:, base + 512:base + 1024],
                                        start=(j == 0), stop=(j == 7))
                    return last
                pe.group(f, pre=(pre,), post=(post,))

        def wout_mm(cc, ps):
            for j2 in range(4):
                j0 = 2 * j2
                slot, pre, post = astr.load(
                    asl[:, (astr.n % 3) * 2 * D:(astr.n % 3 + 1) * 2 * D],
                    wa["Wout"].ap()[j2, :, :, :].rearrange("p kk c -> p (kk c)"))

                def f(e, _sl=slot, _j0=j0):
                    last = None
                    for kk in range(2):
                        j = _j0 + kk
                        base = _sl * 2 * D + kk * D
                        lhsT = yfm[:, j * 128:(j + 1) * 128]
                        e.matmul(ps[:, 0:512], lhsT, asl[:, base:base + 512],
                                 start=(j == 0), stop=(j == 7))
                        last = e.matmul(ps[:, 512:1024], lhsT, asl[:, base + 512:base + 1024],
                                        start=(j == 0), stop=(j == 7))
                    return last
                pe.group(f, pre=(pre,), post=(post,))

        def mm2S(lhsT, rhs0, rhs1, rows=(0, 128)):
            def f(e):
                e.matmul(psS[rows[0]:rows[1], 0:512], lhsT, rhs0, start=True, stop=True)
                return e.matmul(psS[rows[0]:rows[1], 512:1024], lhsT, rhs1, start=True, stop=True)
            pe.group(f)

        FLUSH_PER_GROUP = 6

        def stream_layer(wtname, nI, rhs_sb, evac, xn_need=None):
            wte = wt[wtname]
            for g in range(wte.shape[1]):
                for k2 in range(nI // 2):
                    k0 = 2 * k2
                    slot, pre, post = wstr.load(
                        wsl[:, (wstr.n % 5) * 1024:(wstr.n % 5 + 1) * 1024],
                        wte.ap()[k2, g, :, :, :].rearrange("p kk c -> p (kk c)"))
                    evneed = rec.ev

                    def f(e, _sl=slot, _k0=k0, _nI=nI, _rhs=rhs_sb, _ev=evneed,
                          _xn=xn_need):
                        if _k0 == 0 and _xn:
                            e.wait_ge(tk_sem, _xn)
                        if _ev and _k0 == 0:
                            e.wait_ge(ev_sem, _ev)
                        last = None
                        for kk in range(2):
                            k = _k0 + kk
                            for u in range(4):
                                last = e.matmul(
                                    ps4b[:, u * 512:(u + 1) * 512],
                                    wsl[:, _sl * 1024 + kk * 512 + u * 128:
                                        _sl * 1024 + kk * 512 + (u + 1) * 128],
                                    _rhs[:, k * T:(k + 1) * T],
                                    start=(k == 0), stop=(k == _nI - 1))
                        return last
                    pe.group(f, pre=(pre,), post=(post,), tick=False, tkwait=False)
                    rec.flush(1)
                evac(g)
                rec.flush(FLUSH_PER_GROUP)

        def ffn_pass(src_fm, out_cb, pre_layer=None):
            def hook(i):
                if pre_layer is not None:
                    pre_layer(i)

            def act_evac(dst, g, func, scol):
                def f(e, _g=g):
                    last = None
                    for u in range(4):
                        ch = 4 * _g + u
                        last = e.activation(
                            dst[:, ch * T:(ch + 1) * T],
                            ps4b[:, u * 512:(u + 1) * 512], func,
                            scale=scales[:, scol + ch:scol + ch + 1])
                    return last
                evp(ac, f, pre=((wstr.free_sem, wstr.n),))

            def ev_w1(g):
                act_evac(h_a, g, AF.Silu, 0)
            stream_layer("w1", 8, src_fm, ev_w1, xn_need=XN[0])

            hook(1)

            def ev_w2(g):
                def f(e, _g=g):
                    last = None
                    for u in range(4):
                        ch = 4 * _g + u
                        last = e.scalar_tensor_tensor(
                            h_b[:, ch * T:(ch + 1) * T],
                            ps4b[:, u * 512:(u + 1) * 512],
                            scales[:, 32 + ch:33 + ch],
                            h_a[:, ch * T:(ch + 1) * T],
                            ALU.mult, ALU.mult)
                    return last
                evp(dv, f, pre=((wstr.free_sem, wstr.n), (ev_sem, rec.ev)))
            stream_layer("w2", 8, src_fm, ev_w2, xn_need=XN[0])

            hook(2)

            def ev_h1(g):
                act_evac(h_a, g, AF.Gelu, 64)
            stream_layer("h1", NKH, h_b, ev_h1)

            hook(3)

            def ev_h2(g):
                act_evac(h_b, g, AF.Gelu, 96)
            stream_layer("h2", NKH, h_a, ev_h2)

            hook(4)

            for p2 in range(NCH // 2):
                cc0 = 2 * p2
                for m2 in range(NKH // 2):
                    m0 = 2 * m2
                    slot, pre, post = w3str.load(
                        w3sl[:, (w3str.n % 3) * 2 * D:(w3str.n % 3 + 1) * 2 * D],
                        wt["w3"].ap()[m2, :, :, :].rearrange("p kk c -> p (kk c)"))
                    evneed = rec.ev

                    def f(e, _sl=slot, _m0=m0, _c0=cc0, _ev=evneed):
                        if _ev and _m0 == 0:
                            e.wait_ge(ev_sem, _ev)
                        last = None
                        for kk in range(2):
                            m = _m0 + kk
                            base = _sl * 2 * D + kk * D
                            for ci, ps in ((_c0, psD), (_c0 + 1, ps4b)):
                                lhsT = h_b[:, m * T + ci * 128: m * T + (ci + 1) * 128]
                                e.matmul(ps[:, 0:512], lhsT, w3sl[:, base:base + 512],
                                         start=(m == 0), stop=(m == NKH - 1))
                                last = e.matmul(ps[:, 512:1024], lhsT,
                                                w3sl[:, base + 512:base + 1024],
                                                start=(m == 0), stop=(m == NKH - 1))
                        return last
                    pe.group(f, pre=(pre,), post=(post,), tick=False, tkwait=False)
                out_cb(cc0, psD[:, 0:1024])
                rec.flush(2 * FLUSH_PER_GROUP)
                out_cb(cc0 + 1, ps4b[:, 0:1024])
                rec.flush(2 * FLUSH_PER_GROUP)

        def scan_chunk(cc, own, strip):
            tok0 = strip * T + cc * 128
            gp.dma_start(out=r1d[:, :], in_=r1sc.ap()[tok0:tok0 + 128, :])
            ln_to(xnt[:, :], r1d[:, :])
            transpose_chunk(xn2_fm, xnt, cc)

            def f_z1(e):
                last = None
                for j in range(8):
                    last = e.matmul(psS[0:16, 0:128], agk1_sb[:, j * 16:(j + 1) * 16],
                                    xn2_fm[:, j * T + cc * 128: j * T + (cc + 1) * 128],
                                    start=(j == 0), stop=(j == 7))
                return last
            pe.group(f_z1)
            dv.tensor_copy(z1_bf[:, :], psS[0:16, 0:128])
            mm2S(z1_bf[:, :], agk2_sb[:, 0:512], agk2_sb[:, 512:1024])
            dv.tensor_add(zb[:, :], psS[:, :], bgk_bc[:, :])
            ac.activation(t1[:, :], zb[:, :], AF.Abs)
            ac.activation(t2[:, :], t1[:, :], AF.Exp, scale=-1.0)
            dv.tensor_scalar_add(t1[:, :], t2[:, :], 1.0)
            ac.activation(t2[:, :], t1[:, :], AF.Ln)
            dv.tensor_scalar_min(t1[:, :], zb[:, :], 0.0)
            dv.tensor_sub(g_bf[:, :], t1[:, :], t2[:, :])

            mm2S(tri16[:, :], g_bf[:, 0:512], g_bf[:, 512:1024])
            dv.tensor_copy(cumG[:, :], psS[:, :])
            dv.tensor_copy(mid_bf[:, :], cumG[64:65, :])

            def f_sumg(e):
                e.matmul(psS[0:1, 0:512], sixcol[:, :], g_bf[:, 0:512], start=True, stop=True)
                return e.matmul(psS[0:1, 512:1024], sixcol[:, :], g_bf[:, 512:1024],
                                start=True, stop=True)
            pe.group(f_sumg)
            dv.tensor_sub(xnt[0:1, :], psS[0:1, :], mid_bf[:, :])
            mm2S(onesrow[0:1, :], mid_bf[0:1, 0:512], mid_bf[0:1, 512:1024])
            dv.tensor_sub(cumGc[:, :], cumG[:, :], psS[:, :])
            ac.activation(lam[:, :], cumGc[:, :], AF.Exp)
            ac.activation(dinv[:, :], cumGc[:, :], AF.Exp, scale=-1.0)
            ac.activation(emu_bf[:, :], mid_bf[:, :], AF.Exp)
            ac.activation(f_bf[:, :], xnt[0:1, :], AF.Exp)

            mm2S(onesrow[0:1, 0:64], emu_bf[0:1, 0:512], emu_bf[0:1, 512:1024], rows=(0, 64))
            dv.tensor_mul(S0x[0:64, :], S[:, :], psS[0:64, :])
            dv.tensor_copy(S0x_bf[:, :], S0x[0:64, :])

            aproj(xn2_fm, cc, wa["Wv"], psS)
            dv.tensor_mul(vt_bf[:, :], psS[:, :], dinv[:, :])
            aproj(xn2_fm, cc, wa["Wk"], psS)
            dv.tensor_copy(ktok[:, :], psS[:, :])
            if own:
                dv.tensor_copy(ktf[:, :], psS[:, :])

            def f_ds(e):
                last = None
                for h in range(NH):
                    last = e.matmul(psS[0:64, h * 64:(h + 1) * 64],
                                    ktok[:, h * 64:(h + 1) * 64], vt_bf[:, h * 64:(h + 1) * 64],
                                    start=(h % 8 == 0), stop=(h % 8 == 7))
                return last
            pe.group(f_ds)
            dv.tensor_add(S[:, :], S0x[0:64, :], psS[0:64, :])
            mm2S(onesrow[0:1, 0:64], f_bf[0:1, 0:512], f_bf[0:1, 512:1024], rows=(0, 64))
            dv.tensor_mul(S[:, :], S[:, :], psS[0:64, :])

            if not own:
                return

            def tr8(e, src_sb):
                last = None
                for j in range(8):
                    last = e.transpose(psS[:, j * 128:(j + 1) * 128],
                                       src_sb[:, j * 128:(j + 1) * 128], ident[:, :])
                return last
            pe.group(lambda e: tr8(e, ktf))
            dv.tensor_copy(kfm[0:64, 0:1024].rearrange("p (j c) -> p j c", j=8),
                           psS[0:64, :].rearrange("p (j c) -> p j c", j=8))
            dv.tensor_copy(kfm[0:64, 1024:2048].rearrange("p (j c) -> p j c", j=8),
                           psS[64:128, :].rearrange("p (j c) -> p j c", j=8))
            aproj(xn2_fm, cc, wa["Wq"], psS)
            dv.tensor_copy(qtf[:, :], psS[:, :])
            pe.group(lambda e: tr8(e, qtf))
            dv.tensor_copy(qfm[0:64, 0:1024].rearrange("p (j c) -> p j c", j=8),
                           psS[0:64, :].rearrange("p (j c) -> p j c", j=8))
            dv.tensor_copy(qfm[0:64, 1024:2048].rearrange("p (j c) -> p j c", j=8),
                           psS[64:128, :].rearrange("p (j c) -> p j c", j=8))

            for q4 in range(4):
                def f_a(e, _q=q4):
                    last = None
                    for hh in range(4):
                        h = _q * 4 + hh
                        b = (h >> 1) + 8 * (h & 1)
                        last = e.matmul(psS[:, hh * 128:(hh + 1) * 128],
                                        kfm[0:64, b * 128:(b + 1) * 128],
                                        qfm[0:64, b * 128:(b + 1) * 128],
                                        start=(hh == 0), stop=(hh == 3))
                    return last
                pe.group(f_a)
                dv.tensor_mul(am[:, :], psS[:, 0:512], trirep[:, :])

                def f_oi(e, _q=q4):
                    last = None
                    for hh in range(4):
                        h = _q * 4 + hh
                        b = (h >> 1) + 8 * (h & 1)
                        col = 512 + 64 * hh
                        e.matmul(psS[:, col:col + 64], am[:, hh * 128:(hh + 1) * 128],
                                 vt_bf[:, h * 64:(h + 1) * 64],
                                 start=(hh == 0), stop=False)
                        last = e.matmul(psS[:, col:col + 64],
                                        qfm[0:64, b * 128:(b + 1) * 128],
                                        S0x_bf[0:64, h * 64:(h + 1) * 64],
                                        start=False, stop=(hh == 3))
                    return last
                pe.group(f_oi)
                dv.tensor_mul(out_pre[:, q4 * 256:(q4 + 1) * 256], psS[:, 512:768],
                              lam[:, q4 * 256:(q4 + 1) * 256])

            ln_to(t1[:, :], out_pre[:, :])
            dv.tensor_mul(t2[:, :], t1[:, :], gam_bc[:, :])
            dv.tensor_add(t2[:, :], t2[:, :], bet_bc[:, :])
            aproj(xn2_fm, cc, wa["Wg"], psS)
            ac.activation(t1[:, :], psS[:, :], AF.Silu)
            dv.tensor_mul(ytf[:, :], t2[:, :], t1[:, :])
            pe.group(lambda e: tr8(e, ytf))
            dv.tensor_copy(yfm[:, :].rearrange("p (j c) -> p j c", j=8),
                           psS[:, :].rearrange("p (j c) -> p j c", j=8))
            wout_mm(cc, psS)
            own_tok0 = (strip - NPRE) * T + cc * 128
            gp.dma_start(out=r1d[:, :], in_=r1sc.ap()[tok0:tok0 + 128, :])
            dv.tensor_add(xnt[:, :], r1d[:, :], psS[:, :])
            gp.dma_start(out=r2sc.ap()[own_tok0:own_tok0 + 128, :], in_=xnt[:, :])

        def record_ffn1(s, pre_layer=None):
            gp.group(lambda e, _s=s: e.dma_start(
                out=x_sb[:, :].rearrange("p (c d) -> p c d", c=NCH),
                in_=xin.ap().rearrange("(s c p) d -> p s c d", s=NSTRIP, p=128)[:, _s, :, :]),
                pre=((ev_sem, rec.ev),))
            for cc in range(NCH):
                ln_to(xnt[:, :], x_sb[:, cc * D:(cc + 1) * D],
                      ev_guard=(cc == 0))
                transpose_chunk(xn_fm, xnt, cc)
            XN[0] = rec.val

            def out1(cc, psrc, _s=s):
                dv.waitsem(w3str.free_sem, w3str.n)
                evp(dv, lambda e, _cc=cc, _p=psrc: e.tensor_add(
                    r1c[:, :], _p, x_sb[:, _cc * D:(_cc + 1) * D]))
                tok0 = _s * T + cc * 128
                gp.group(lambda e, _t=tok0: e.dma_start(
                    out=r1sc.ap()[_t:_t + 128, :], in_=r1c[:, :]),
                    pre=((ev_sem, rec.ev),))

            ffn_pass(xn_fm, out1, pre_layer=pre_layer)

        def record_ffn2(s):
            for cc in range(NCH):
                own_tok0 = (s - NPRE) * T + cc * 128
                gp.group(lambda e, _t=own_tok0: e.dma_start(
                    out=r1c[:, :], in_=r2sc.ap()[_t:_t + 128, :]),
                    pre=((ev_sem, rec.ev),))
                ln_to(xnt[:, :], r1c[:, :], ev_guard=(cc == 0))
                transpose_chunk(xn_fm, xnt, cc)
            XN[0] = rec.val

            def out2(cc, psrc, _s=s):
                dv.waitsem(w3str.free_sem, w3str.n)
                evp(dv, lambda e, _p=psrc: e.tensor_copy(w3tmp[:, :], _p))
                own_tok0 = (_s - NPRE) * T + cc * 128
                gp.group(lambda e, _t=own_tok0: e.dma_start(
                    out=r1c[:, :], in_=r2sc.ap()[_t:_t + 128, :]),
                    pre=((ev_sem, rec.ev),))
                evp(dv, lambda e: e.tensor_add(xnt[:, :], r1c[:, :], w3tmp[:, :]),
                    pre=((ev_sem, rec.ev),))
                gp.group(lambda e, _t=own_tok0: e.dma_start(
                    out=out_ext.ap()[_t:_t + 128, :], in_=xnt[:, :]),
                    pre=((ev_sem, rec.ev),))

            ffn_pass(xn_fm, out2)

        prev_own = False
        for s in range(NSTRIP):
            own = s >= NPRE
            record_ffn1(s, pre_layer=prep_hook if s == 0 else None)
            rec.flush_all()
            if prev_own:
                record_ffn2(s - 1)
            rec.defer = True
            for cc in range(NCH):
                scan_chunk(cc, own, s)
            rec.defer = False
            prev_own = own
        rec.flush_all()
        record_ffn2(NSTRIP - 1)
        rec.flush_all()

        final_ticket = rec.val
        final_ev = rec.ev

        def replay(engname):
            skip_self = engname == "pe"

            def f(eng):
                low = 0
                qcnt = 0
                for (w, fn, inc, pre, post) in rec.items[engname]:
                    if w is not None and w > low:
                        eng.wait_ge(tk_sem, w)
                        low = w
                    for (sem, val) in pre:
                        eng.wait_ge(sem, val)
                    inst = fn(eng)
                    if inc == 16:
                        qs = qsems[engname]
                        assert inst is not None
                        inst.then_inc(qs, 16)
                        qcnt += 16
                        eng.wait_ge(qs, qcnt)
                        eng.nop().then_inc(tk_sem, 16)
                        for (sem, val) in post:
                            eng.nop().then_inc(sem, val)
                    elif inc or post:
                        assert inst is not None, f"group must return an instruction ({engname})"
                        updates = ([(tk_sem, inc)] if inc else []) + list(post)
                        inst.then_inc(*updates[0])
                        for (sem, val) in updates[1:]:
                            eng.nop().then_inc(sem, val)
                    if skip_self:
                        low += inc
                if engname == "gp":
                    eng.wait_ge(tk_sem, final_ticket)
                    eng.wait_ge(ev_sem, final_ev)
            return f

        block.sync(replay("sync"))
        block.gpsimd(replay("gp"))
        block.tensor(replay("pe"))
        block.vector(replay("dv"))
        block.scalar(replay("ac"))
        for k in list(rec.items):
            rec.items[k] = []
    return nc


def _consts(cfg):
    import ml_dtypes
    tri = np.tril(np.ones((128, 128), dtype=np.float32)).T
    bf = lambda a: a.astype(ml_dtypes.bfloat16)
    return dict(
        tri16_c=bf(tri / 16.0),
        trirep_c=bf(np.tile(tri, (1, 4))),
        sixcol_c=bf(np.full((128, 1), 1 / 16.0, dtype=np.float32)),
        onesrow_c=bf(np.ones((1, 128), dtype=np.float32)),
        ident_c=np.eye(128, dtype=np.float32),
    )


def make_in_maps(cfg, x_full, W):
    T, NSTRIP, NOWN = cfg["T"], cfg["n_strips"], cfg["own_strips"]
    SEQ = NSTRIP * T
    SLAB = NOWN * T
    consts = _consts(cfg)
    f32 = np.float32

    def fm_scales(s):
        return np.ascontiguousarray(s.reshape(-1, 128).T.astype(f32))

    scales = np.concatenate([fm_scales(W["ffn_w1_s"]), fm_scales(W["ffn_w2_s"]),
                             fm_scales(W["ffn_h1_s"]), fm_scales(W["ffn_h2_s"]),
                             fm_scales(W["ffn_w3_s"])], axis=1)
    bcast = lambda v: np.ascontiguousarray(np.broadcast_to(v.astype(f32), (128, D)))
    shared = dict(
        Wq=W["attn_Wq"], Wk=W["attn_Wk"], Wv=W["attn_Wv"], Wg=W["attn_Wg"],
        Wout=W["attn_Wout"], Wgk1=W["attn_Wgk1"], Wgk2=W["attn_Wgk2"],
        w1=W["ffn_w1_w"], w2=W["ffn_w2_w"], h1=W["ffn_h1_w"], h2=W["ffn_h2_w"],
        w3=W["ffn_w3_w"],
        scales=scales,
        bgk_bc=bcast(W["attn_bgk2"]),
        gam_bc=bcast(W["attn_gamma"]),
        bet_bc=bcast(W["attn_beta"]),
        **consts,
    )
    in_maps = []
    for core in range(8):
        b, p = divmod(core, 4)
        take = (p + 1) * SLAB
        xi = np.zeros((SEQ, D), dtype=f32)
        xi[SEQ - take:] = x_full[b, :take]
        m = dict(shared)
        m["xin"] = xi
        in_maps.append(m)
    return in_maps


def kernel(**inputs):
    cfg = CFG_FULL
    W = {
        "attn_Wq": np.asarray(inputs["L0_attn_Wq"]),
        "attn_Wk": np.asarray(inputs["L0_attn_Wk"]),
        "attn_Wv": np.asarray(inputs["L0_attn_Wv"]),
        "attn_Wg": np.asarray(inputs["L0_attn_Wg"]),
        "attn_Wout": np.asarray(inputs["L0_attn_Wout"]),
        "attn_Wgk1": np.asarray(inputs["L0_attn_Wgk1"]),
        "attn_Wgk2": np.asarray(inputs["L0_attn_Wgk2"]),
        "attn_bgk2": np.asarray(inputs["L0_attn_bgk2"]),
        "attn_gamma": np.asarray(inputs["L0_attn_gamma"]),
        "attn_beta": np.asarray(inputs["L0_attn_beta"]),
        "ffn_w1_w": np.asarray(inputs["L0_ffn_w1_w"]),
        "ffn_w1_s": np.asarray(inputs["L0_ffn_w1_s"]),
        "ffn_w2_w": np.asarray(inputs["L0_ffn_w2_w"]),
        "ffn_w2_s": np.asarray(inputs["L0_ffn_w2_s"]),
        "ffn_h1_w": np.asarray(inputs["L0_ffn_h1_w"]),
        "ffn_h1_s": np.asarray(inputs["L0_ffn_h1_s"]),
        "ffn_h2_w": np.asarray(inputs["L0_ffn_h2_w"]),
        "ffn_h2_s": np.asarray(inputs["L0_ffn_h2_s"]),
        "ffn_w3_w": np.asarray(inputs["L0_ffn_w3_w"]),
        "ffn_w3_s": np.asarray(inputs["L0_ffn_w3_s"]),
    }
    x_full = np.asarray(inputs["L0_attn_x"], dtype=np.float32)
    in_maps = make_in_maps(cfg, x_full, W)
    del x_full, W
    import gc
    gc.collect()
    nc = build_program(cfg)
    res = run_bass_kernel_spmd(nc, in_maps, core_ids=list(range(8)))
    SLAB = cfg["own_strips"] * cfg["T"]
    out = np.empty((2, 16384, D), dtype=np.float32)
    for core in range(8):
        b, p = divmod(core, 4)
        out[b, p * SLAB:(p + 1) * SLAB] = np.asarray(res.results[core]["out"])
    return out
